# version 15 (final)
# baseline (speedup 1.0000x reference)
from concourse.bass_utils import run_bass_kernel_spmd
"""Raw-Bass emission framework with explicit dependency-tracked semaphores.

Model:
- One semaphore per compute engine (in-order completion: sem value k ==
  first k ops done).
- DMA queues (sync/scalar/gpsimd): completions are NOT ordered within a
  queue, so each queue rotates over NDS semaphores and each DMA first
  waits for the previous DMA on its own semaphore (issue-gating) --
  making each DMA semaphore's completions ordered. +16 per DMA.
- Ops declare reads/writes as region keys; framework computes RAW/WAR/WAW
  waits (same-engine included -- probed racy) and dedups.
- Two-phase: record ops eagerly, then replay into nc.Block() sections.
"""
import numpy as np
import ml_dtypes
import concourse.bass as bass
import concourse.mybir as mybir

F32 = mybir.dt.float32
BF16 = mybir.dt.bfloat16
AF = mybir.ActivationFunctionType
ALU = mybir.AluOpType
AX = mybir.AxisListType

ENGINES = ("sync", "scalar", "vector", "tensor", "gpsimd")
DMA_QUEUES = ("sync", "scalar", "gpsimd")
NDS = 4


class Prog:
    def __init__(self, nc):
        self.nc = nc
        self.sem = {}
        self.cnt = {}
        for e in ENGINES:
            self._new_sem(f"e_{e}")
        for q in DMA_QUEUES:
            for i in range(NDS):
                self._new_sem(f"d_{q}{i}")
        self.dma_idx = {q: 0 for q in DMA_QUEUES}
        self.ops = []
        self.writer = {}
        self.readers = {}
        self.waited = {e: {} for e in ENGINES}

    def _new_sem(self, name):
        self.sem[name] = self.nc.alloc_semaphore(name)
        self.cnt[name] = 0

    def _emit(self, eng, fn, waits, inc_sem, inc, extra_gate=None):
        self.ops.append((eng, fn, waits, inc_sem, inc))

    def op(self, eng, fn, reads=(), writes=(), dma=False):
        assert not dma or eng in DMA_QUEUES
        waits = {}

        def add_wait(sname, v):
            if v > waits.get(sname, 0):
                waits[sname] = v
        for k in reads:
            w = self.writer.get(k)
            if w:
                add_wait(*w)
        for k in writes:
            w = self.writer.get(k)
            if w:
                add_wait(*w)
            for sname, rc in self.readers.get(k, {}).items():
                add_wait(sname, rc)
        if dma:
            i = self.dma_idx[eng] % NDS
            self.dma_idx[eng] += 1
            sname = f"d_{eng}{i}"
            if self.cnt[sname] > 0:
                add_wait(sname, self.cnt[sname])
            inc = 16
        else:
            sname = f"e_{eng}"
            inc = 1
        wd = self.waited[eng]
        for f in list(waits):
            if (eng == "tensor" and f == "e_tensor") or wd.get(f, 0) >= waits[f]:
                del waits[f]
            else:
                wd[f] = waits[f]
        self.cnt[sname] += inc
        after = self.cnt[sname]
        for k in writes:
            self.writer[k] = (sname, after)
            self.readers.pop(k, None)
        for k in reads:
            d = self.readers.setdefault(k, {})
            if after > d.get(sname, 0):
                d[sname] = after
        self._emit(eng, fn, waits, sname, inc)
        return sname, after

    def barrier(self):
        snapshot = dict(self.cnt)
        nopc = {}
        for e in ENGINES:
            waits = {}
            sown = f"e_{e}"
            if snapshot[sown] > 0:
                waits[sown] = snapshot[sown]
            for q in DMA_QUEUES:
                if q == e:
                    for i in range(NDS):
                        sq = f"d_{q}{i}"
                        if snapshot[sq] > 0 and self.waited[e].get(sq, 0) < snapshot[sq]:
                            waits[sq] = snapshot[sq]
                            self.waited[e][sq] = snapshot[sq]
            self.cnt[sown] += 1
            nopc[e] = self.cnt[sown]
            self.ops.append((e, lambda en: en.nop(nofuse=True), waits, sown, 1))
        for e in ENGINES:
            waits = {}
            wd = self.waited[e]
            for f in ENGINES:
                if f != e and wd.get(f"e_{f}", 0) < nopc[f]:
                    waits[f"e_{f}"] = nopc[f]
                    wd[f"e_{f}"] = nopc[f]
            for q in DMA_QUEUES:
                for i in range(NDS):
                    sq = f"d_{q}{i}"
                    if snapshot[sq] > 0 and wd.get(sq, 0) < snapshot[sq]:
                        waits[sq] = snapshot[sq]
                        wd[sq] = snapshot[sq]
            self.cnt[f"e_{e}"] += 1
            self.ops.append((e, lambda en: en.nop(nofuse=True), waits, f"e_{e}", 1))
        self.writer = {}
        self.readers = {}

    def finalize(self, block):
        self.barrier()
        ops = self.ops
        sems = self.sem

        def mk(name):
            def closure(e):
                for eng, fn, waits, sname, inc in ops:
                    if eng != name:
                        continue
                    for f, v in waits.items():
                        e.wait_ge(sems[f], v)
                    fn(e).then_inc(sems[sname], inc)
            return closure

        block.sync(mk("sync"))
        block.scalar(mk("scalar"))
        block.vector(mk("vector"))
        block.tensor(mk("tensor"))
        block.gpsimd(mk("gpsimd"))

    def dma(self, eng, out, in_, reads=(), writes=(), **kw):
        return self.op(eng, lambda e: e.dma_start(out=out, in_=in_, **kw),
                       reads=reads, writes=writes, dma=True)

    def mm(self, psum_ap, lhsT, rhs, start, stop, reads=(), writes=()):
        return self.op("tensor",
                       lambda e: e.matmul(psum_ap, lhsT, rhs, start=start, stop=stop),
                       reads=reads, writes=writes)

    def mm_acc(self, psum_ap, pairs, reads, writes):
        n = len(pairs)
        for i, (lt, rh) in enumerate(pairs):
            self.mm(psum_ap, lt, rh, start=(i == 0), stop=(i == n - 1),
                    reads=reads if (i == 0 or i == n - 1) else (),
                    writes=writes if (i == 0 or i == n - 1) else ())

    def transpose(self, psum_ap, in_ap, ident_ap, reads=(), writes=()):
        return self.op("tensor",
                       lambda e: e.transpose(psum_ap, in_ap, ident_ap),
                       reads=reads, writes=writes)


def make_consts(nc):
    consts = {}
    consts["id_bf"] = nc.inline_tensor(np.eye(128, dtype=ml_dtypes.bfloat16), "c_id_bf")
    consts["id_f32"] = nc.inline_tensor(np.eye(128, dtype=np.float32), "c_id_f32")
    consts["ones_bf"] = nc.inline_tensor(np.ones((128, 128), dtype=ml_dtypes.bfloat16), "c_ones_bf")
    return consts


"""Sublayer emitters for the mk_hybrid_d2 megakernel.

Conventions (per core):
- Residual stream x in DRAM, feature-major [1024, 4096] fp32, ping-pong.
- Matmuls: bf16 operands, fp32 PSUM; out = lhsT.T @ rhs (contract over
  partitions). Weights [K, M] in DRAM load directly as lhsT chunks.
- LN over features via ones[128,128] matmuls (sum replicated over partitions).
- PSUM pool (8 banks): PS[0..5] fp32 [128,512], PSB bf16 [128,1024].
"""
import numpy as np
from contextlib import ExitStack
import concourse.bass as bass

S = 4096
D = 1024
NCH = 8
NTG = 8
TG = 512
EPS = 1e-5


def ts(i, n=128):
    return slice(i * n, (i + 1) * n)


def APx(handle, offset, dims):
    return bass.AP(handle, offset, dims)


class Ctx:
    def __init__(self, nc, p, stack):
        self.nc = nc
        self.p = p
        cd = make_consts(nc)
        self.id_bf = stack.enter_context(nc.sbuf_tensor("k_id_bf", [128, 128], BF16))
        self.id_f32 = stack.enter_context(nc.sbuf_tensor("k_id_f32", [128, 128], F32))
        self.ones_bf = stack.enter_context(nc.sbuf_tensor("k_ones_bf", [128, 128], BF16))
        p.dma("sync", self.id_bf[:], cd["id_bf"].ap(), writes=[("k_id_bf", "*")])
        p.dma("sync", self.id_f32[:], cd["id_f32"].ap(), writes=[("k_id_f32", "*")])
        p.dma("sync", self.ones_bf[:], cd["ones_bf"].ap(), writes=[("k_ones_bf", "*")])
        self.PS = [stack.enter_context(nc.psum_tensor(f"k_ps{i}", [128, TG], F32))
                   for i in range(6)]
        self.PSBK = [stack.enter_context(nc.psum_tensor(f"k_psb{i}", [128, 1024], BF16))
                     for i in range(2)]
        self.vstg = stack.enter_context(nc.sbuf_tensor("k_vstg", [128, 128], F32))
        self.KID = ("k_id_f32", "*")
        self.KIDB = ("k_id_bf", "*")
        self.KONE = ("k_ones_bf", "*")

    def ps(self, i):
        return self.PS[i], ("PS", i)

    def psb(self, slot):
        i = slot % 2
        return self.PSBK[i][:, 0:128], ("PSB", i)

    def batch_transpose(self, p, srcs, src_keys, dst, dst_key, bank, eng="vector"):
        pb = self.PSBK[bank % 2]
        kpb = ("PSB", bank % 2)
        n = len(srcs)
        for j, sap in enumerate(srcs):
            p.transpose(pb[:, j * 128:(j + 1) * 128], sap, self.id_bf[:],
                        reads=(list(src_keys) + [self.KIDB, kpb]) if j == 0 else [],
                        writes=[kpb] if j in (0, n - 1) else [])
        if eng == "scalar":
            p.op(eng, lambda e: e.copy(dst, pb[:, 0:n * 128]),
                 reads=[kpb], writes=[dst_key])
        else:
            p.op(eng, lambda e: e.tensor_copy(dst, pb[:, 0:n * 128]),
                 reads=[kpb], writes=[dst_key])


def load_vecT(ctx, name, handle, base, n, stack, dtype=F32):
    nc, p = ctx.nc, ctx.p
    m = n // 128
    stg = ctx.vstg
    out = stack.enter_context(nc.sbuf_tensor(name, [128, m], dtype))
    pt, kpt = ctx.ps(5)
    key = (name, "*")
    p.dma("sync", stg[0:m, 0:128], APx(handle, base, [[128, m], [1, 128]]),
          reads=[("k_vstg", "*")], writes=[("k_vstg", "*")])
    p.transpose(pt[0:128, 0:m], stg[0:m, 0:128], ctx.id_f32[0:m, 0:m],
                reads=[("k_vstg", "*"), ctx.KID], writes=[kpt])
    p.op("vector", lambda e: e.tensor_copy(out[:], pt[0:128, 0:m]),
         reads=[kpt], writes=[key])
    return out, key


def load_col8(ctx, name, handle, base, stack):
    nc, p = ctx.nc, ctx.p
    out = stack.enter_context(nc.sbuf_tensor(name, [8, 1], F32))
    key = (name, "*")
    p.dma("sync", out[:], APx(handle, base, [[1, 8], [1, 1]]), writes=[key])
    return out, key


def emit_ln_prologue(ctx, xin_ap, stack, pre, gamma=None, beta=None,
                     fold_double=False, xin_key="XIN", spill=None, gate=None):
    nc, p = ctx.nc, ctx.p
    if spill is None:
        xn = stack.enter_context(nc.sbuf_tensor(f"{pre}_xn", [128, NCH * S], BF16))
    else:
        xst = stack.enter_context(nc.sbuf_tensor(f"{pre}_xst", [128, 2 * TG], BF16))
    xb = stack.enter_context(nc.sbuf_tensor(f"{pre}_xb", [128, NCH * TG], BF16))
    xf = stack.enter_context(nc.sbuf_tensor(f"{pre}_xf", [128, 2 * TG], F32))
    sq = stack.enter_context(nc.sbuf_tensor(f"{pre}_sq", [128, 2 * TG], BF16))
    stat = stack.enter_context(nc.sbuf_tensor(f"{pre}_st", [128, 3 * TG], F32))
    epsc = stack.enter_context(nc.sbuf_tensor(f"{pre}_eps", [128, 1], F32))
    ps0, kps0 = ctx.ps(0)
    ps1, kps1 = ctx.ps(1)
    ones = ctx.ones_bf

    a_scale = 1.0 + EPS if fold_double else 1.0
    b_bias = EPS * EPS if fold_double else EPS
    p.op("vector", lambda e: e.memset(epsc[:], b_bias), writes=[(f"{pre}eps", "*")])
    if gate is not None:
        g8 = stack.enter_context(nc.sbuf_tensor(f"{pre}_g8", [8, TG], F32))
        psg_t, kpsg = ctx.ps(2)
        psg8 = psg_t[0:8, :]
        pstr_t, kpstr = ctx.ps(3)

    mu, msq = stat[:, ts(0, TG)], stat[:, ts(1, TG)]
    var, rr = stat[:, ts(1, TG)], stat[:, ts(2, TG)]
    for tg in range(NTG):
        kxf = lambda c: (f"{pre}xf", c % 2)
        kxb = lambda c: (f"{pre}xb", c)
        ksq = lambda c: (f"{pre}sq", c % 2)
        for c in range(NCH):
            xfv = xf[:, ts(c % 2, TG)]
            xbv = xb[:, ts(c, TG)]
            sqv = sq[:, ts(c % 2, TG)]
            p.dma("sync", xfv, xin_ap[ts(c), ts(tg, TG)],
                  reads=[(xin_key, "*")], writes=[kxf(c)])
            p.op("vector", lambda e, a=xbv, b=xfv: e.tensor_copy(a, b),
                 reads=[kxf(c)], writes=[kxb(c)])
            p.op("gpsimd", lambda e, a=sqv, b=xbv: e.tensor_mul(a, b, b),
                 reads=[kxb(c)], writes=[ksq(c)])
            p.mm(ps0[:], ones[:], xbv, start=(c == 0), stop=(c == NCH - 1),
                 reads=[kxb(c), ctx.KONE], writes=[kps0] if c in (0, NCH - 1) else ())
            p.mm(ps1[:], ones[:], sqv, start=(c == 0), stop=(c == NCH - 1),
                 reads=[ksq(c), ctx.KONE], writes=[kps1] if c in (0, NCH - 1) else ())
            if gate is not None:
                p.mm(psg8, gate["gwf"][:, ts(c, 8)], xfv, start=(c == 0), stop=(c == NCH - 1),
                     reads=[kxf(c), gate["kgw"]], writes=[kpsg] if c in (0, NCH - 1) else ())
        kmu, kvar, krr = (f"{pre}mu", 0), (f"{pre}var", 0), (f"{pre}rr", 0)
        p.op("vector", lambda e: e.tensor_scalar_mul(mu, ps0[:], 1.0 / D),
             reads=[kps0], writes=[kmu])
        p.op("vector", lambda e: e.tensor_mul(msq, mu, mu),
             reads=[kmu], writes=[(f"{pre}msq", 0)])
        p.op("vector", lambda e: e.scalar_tensor_tensor(var, ps1[:], 1.0 / D, msq,
                                                        op0=ALU.mult, op1=ALU.subtract),
             reads=[kps1, (f"{pre}msq", 0)], writes=[kvar])
        p.op("scalar", lambda e: e.activation(rr, var, AF.Ln, bias=epsc[:], scale=a_scale),
             reads=[kvar, (f"{pre}eps", "*")], writes=[krr])
        p.op("scalar", lambda e: e.activation(rr, rr, AF.Exp, scale=-0.5),
             reads=[krr], writes=[krr])
        if gate is not None:
            kg8 = (f"{pre}g8", 0)
            mu8, rr8 = mu[0:8, :], rr[0:8, :]
            p.op("vector", lambda e: e.tensor_scalar_mul(g8[:], mu8, gate["csG"]),
                 reads=[kmu, gate["kcsG"]], writes=[kg8])
            p.op("vector", lambda e: e.tensor_sub(g8[:], psg8, g8[:]),
                 reads=[kpsg, kg8], writes=[kg8])
            p.op("vector", lambda e: e.tensor_mul(g8[:], g8[:], rr8),
                 reads=[kg8, krr], writes=[kg8])
            p.op("scalar", lambda e: e.activation(g8[:], g8[:], AF.Sigmoid, bias=gate["gb"][:]),
                 reads=[kg8, gate["kgb"]], writes=[kg8])
            for i4 in range(TG // 128):
                tt = tg * (TG // 128) + i4
                p.transpose(pstr_t[0:128, 0:8], g8[0:8, ts(i4)], ctx.id_f32[0:8, 0:8],
                            reads=[kg8, ctx.KID], writes=[kpstr])
                p.op("vector", lambda e, a_=gate["sct"][:, tt * 8:(tt + 1) * 8]:
                     e.tensor_copy(a_, pstr_t[0:128, 0:8]),
                     reads=[kpstr], writes=[("SCT", tt)])
        for c in range(NCH):
            xbv = xb[:, ts(c, TG)]
            tmp = xf[:, ts(c % 2, TG)]
            if spill is None:
                xnv_ = xn[:, c * S + tg * TG: c * S + (tg + 1) * TG]
            else:
                xnv_ = xst[:, ts(c % 2, TG)]
            kxn = (f"{pre}xn", (tg, c)) if spill is None else (f"{pre}xst", c % 2)
            p.op("vector", lambda e, a=tmp, b=xbv: e.tensor_sub(a, b, mu),
                 reads=[kxb(c), kmu], writes=[kxf(c)])
            if gamma is None:
                p.op("vector", lambda e, a=xnv_, b=tmp: e.tensor_mul(a, b, rr),
                     reads=[kxf(c), krr], writes=[kxn])
            else:
                gT, kg = gamma
                bT, kb = beta
                p.op("vector", lambda e, a=tmp: e.tensor_mul(a, a, rr),
                     reads=[kxf(c), krr], writes=[kxf(c)])
                p.op("vector", lambda e, a=xnv_, b=tmp, g=gT[:, c:c + 1], bb=bT[:, c:c + 1]:
                     e.tensor_scalar(a, b, g, bb, op0=ALU.mult, op1=ALU.add),
                     reads=[kxf(c), kg, kb], writes=[kxn])
            if spill is not None:
                p.dma("scalar", APx(spill, (c * 128) * S + tg * TG, [[S, 128], [1, TG]]),
                      xnv_, reads=[kxn], writes=[("XND", (tg, c))])
    return None if spill is not None else xn


def xnv(xn, c, t0, tlen):
    return xn[:, c * S + t0: c * S + t0 + tlen]


def emit_residual_store(ctx, pre, kidx, y_ap, y_key, xin_ap, xout_ap, hc, t0, tlen,
                        bufs, xin_key="XIN", xin_keys=None, xq="gpsimd"):
    p = ctx.p
    xs, ys = bufs
    kx, ky = (f"{pre}epx", kidx), (f"{pre}epy", kidx)
    p.dma(xq, xs[:, 0:tlen], xin_ap[ts(hc), t0:t0 + tlen],
          reads=xin_keys if xin_keys is not None else [(xin_key, "*")], writes=[kx])
    p.op("vector", lambda e: e.tensor_add(ys[:, 0:tlen], y_ap, xs[:, 0:tlen]),
         reads=[kx, y_key], writes=[ky])
    p.dma("sync", xout_ap[ts(hc), t0:t0 + tlen], ys[:, 0:tlen],
          reads=[ky], writes=[("XOUT", (hc, t0))])



def emit_ffn(ctx, xin_ap, xout_ap, w, stack, pre, variant, xin_key="XIN"):
    nc, p = ctx.nc, ctx.p
    idf = ctx.id_f32

    gwf = stack.enter_context(nc.sbuf_tensor(f"{pre}_gwf", [128, 64], F32))
    sct = stack.enter_context(nc.sbuf_tensor(f"{pre}_sct", [128, 256], F32))
    csG = stack.enter_context(nc.sbuf_tensor(f"{pre}_csG", [8, 1], F32))
    onesf = stack.enter_context(nc.sbuf_tensor(f"{pre}_onef", [128, 1], F32))
    gb, kgb = load_col8(ctx, f"{pre}_gb", w["gate_b"], 0, stack)
    p.dma("sync", gwf[:].rearrange("p (c e) -> p c e", e=8),
          APx(w["gate_w"], 0, [[8, 128], [1024, 8], [1, 8]]),
          writes=[(f"{pre}gwf", "*")])
    p.op("vector", lambda e: e.memset(onesf[:], 1.0), writes=[(f"{pre}onef", "*")])
    pcs_t, kpcs = ctx.ps(4)
    p.mm_acc(pcs_t[0:8, 0:1], [(gwf[:, ts(c, 8)], onesf[:]) for c in range(NCH)],
             reads=[(f"{pre}gwf", "*"), (f"{pre}onef", "*")], writes=[kpcs])
    p.op("vector", lambda e: e.tensor_copy(csG[:], pcs_t[0:8, 0:1]),
         reads=[kpcs], writes=[(f"{pre}csG", "*")])
    gate = dict(gwf=gwf, kgw=(f"{pre}gwf", "*"), csG=csG[:], kcsG=(f"{pre}csG", "*"),
                gb=gb, kgb=kgb, sct=sct)

    xnd = nc.dram_tensor(f"{pre}_xnd", [D * S], BF16, kind="Internal")
    emit_ln_prologue(ctx, xin_ap, stack, pre + "L", xin_key=xin_key, spill=xnd,
                     gate=gate)

    NQ = 4
    QT = S // NQ
    nsg = QT // TG
    xq = stack.enter_context(nc.sbuf_tensor(f"{pre}_xq", [128, NCH * QT], BF16))

    def load_xq(q, kq):
        p.dma("sync", xq[:].rearrange("p (c t) -> p c t", t=QT),
              APx(xnd, q * QT, [[S, 128], [128 * S, NCH], [1, QT]]),
              reads=[("XND", (tg_, c_)) for tg_ in range(q * nsg, (q + 1) * nsg)
                     for c_ in range(NCH)],
              writes=[kq])

    def xqv(c, sg):
        return xq[:, c * QT + sg * TG: c * QT + (sg + 1) * TG]

    wtok = stack.enter_context(nc.sbuf_tensor(f"{pre}_wtok", [128, 8], F32))
    mv = stack.enter_context(nc.sbuf_tensor(f"{pre}_mv", [128, 16], F32))
    mr = stack.enter_context(nc.sbuf_tensor(f"{pre}_mr", [128, 16], F32))
    m12 = stack.enter_context(nc.sbuf_tensor(f"{pre}_m12", [128, 4], F32))
    wst = stack.enter_context(nc.sbuf_tensor(f"{pre}_wst", [8, 128], F32))
    wrep = nc.dram_tensor(f"{pre}_wrep", [8 * S], F32, kind="Internal")
    pst_t, kpst = ctx.ps(1)
    scl = 0.1 if variant == "L0" else 1.0
    for tt in range(S // 128):
        tm = sct[:, tt * 8:(tt + 1) * 8]
        ktm = ("SCT", tt)
        k = lambda s_: (f"{pre}{s_}", 0)
        m1, m2, den = m12[:, 0:1], m12[:, 1:2], m12[:, 2:3]
        mv1, mv2 = mv[:, 0:8], mv[:, 8:16]
        mr1, mr2 = mr[:, 0:8], mr[:, 8:16]
        p.op("vector", lambda e: e.memset(mv1, -3e38), writes=[k("mv1")])
        p.op("vector", lambda e: e.memset(mv2, -3e38), writes=[k("mv2")])
        p.op("vector", lambda e, t_=tm: e.reduce_max(m1, t_, axis=AX.X),
             reads=[ktm], writes=[k("m1")])
        p.op("vector", lambda e: e.tensor_copy(mv1[:, 0:1], m1),
             reads=[k("m1")], writes=[k("mv1")])
        p.op("vector", lambda e, t_=tm: e.match_replace(mr1, mv1, t_, -1e30),
             reads=[k("mv1"), ktm], writes=[k("mr1")])
        p.op("vector", lambda e: e.reduce_max(m2, mr1, axis=AX.X),
             reads=[k("mr1")], writes=[k("m2")])
        p.op("vector", lambda e: e.tensor_copy(mv2[:, 0:1], m2),
             reads=[k("m2")], writes=[k("mv2")])
        p.op("vector", lambda e: e.match_replace(mr2, mv2, mr1, -1e30),
             reads=[k("mv2"), k("mr1")], writes=[k("mr2")])
        p.op("vector", lambda e: e.tensor_scalar(mr1, mr2, -1e29, None, op0=ALU.is_lt),
             reads=[k("mr2")], writes=[k("mr1")])
        p.op("vector", lambda e: e.tensor_add(den, m1, m2),
             reads=[k("m1"), k("m2")], writes=[k("den")])
        p.op("vector", lambda e: e.tensor_scalar_add(den, den, 1e-6),
             reads=[k("den")], writes=[k("den")])
        p.op("vector", lambda e: e.reciprocal(den, den),
             reads=[k("den")], writes=[k("den")])
        p.op("vector", lambda e, t_=tm: e.tensor_mul(mr2, mr1, t_),
             reads=[k("mr1"), ktm], writes=[k("mr2")])
        p.op("vector", lambda e, s_=scl: e.tensor_scalar(wtok[:], mr2, den, s_,
                                                         op0=ALU.mult, op1=ALU.mult),
             reads=[k("mr2"), k("den")], writes=[k("wtok")])
        p.transpose(pst_t[0:8, 0:128], wtok[:], idf[:],
                    reads=[k("wtok"), ctx.KID], writes=[kpst])
        p.op("vector", lambda e: e.tensor_copy(wst[:], pst_t[0:8, 0:128]),
             reads=[kpst], writes=[k("wst")])
        p.dma("sync", APx(wrep, tt * 128, [[S, 8], [1, 128]]), wst[:],
              reads=[k("wst")], writes=[(f"{pre}wrep", tt)])

    ubTs = [load_vecT(ctx, f"{pre}ubT{e}", w["up_b"], e * 4096, 4096, stack) for e in range(8)]
    dbTs = [load_vecT(ctx, f"{pre}dbT{e}", w["down_b"], e * 1024, 1024, stack) for e in range(8)]
    subT = load_vecT(ctx, f"{pre}subT", w["sb_up"], 0, 4096, stack)
    sdbT = load_vecT(ctx, f"{pre}sdbT", w["sb_down"], 0, 1024, stack)

    acc = stack.enter_context(nc.sbuf_tensor(f"{pre}_acc", [128, NCH * QT], F32))
    hact = stack.enter_context(nc.sbuf_tensor(f"{pre}_hact", [128, 32 * QT], BF16))
    ucb = stack.enter_context(nc.sbuf_tensor(f"{pre}_ucb", [128, 3 * 1024], BF16))
    dcb = stack.enter_context(nc.sbuf_tensor(f"{pre}_dcb", [128, 2 * 4096], BF16))
    wbc = stack.enter_context(nc.sbuf_tensor(f"{pre}_wbc", [128, QT], F32))
    ytmp = stack.enter_context(nc.sbuf_tensor(f"{pre}_ytmp", [128, TG], F32))
    epx = stack.enter_context(nc.sbuf_tensor(f"{pre}_epx", [128, TG], F32))
    epy = stack.enter_context(nc.sbuf_tensor(f"{pre}_epy", [128, TG], F32))
    tmT = stack.enter_context(nc.sbuf_tensor(f"{pre}_tmT", [128, 1024], F32))
    mx = stack.enter_context(nc.sbuf_tensor(f"{pre}_mx", [128, 2], F32))

    kwrep_all = [(f"{pre}wrep", tt) for tt in range(S // 128)]
    for q in range(NQ):
        load_xq(q, (f"{pre}xq", 0))
        for e_ in [8] + list(range(8)):
            shared = (e_ == 8)
            ubT, kub_bias = (subT if shared else ubTs[e_])
            dbT, kdb_bias = (sdbT if shared else dbTs[e_])
            if not shared:
                p.dma("sync", wbc[:], APx(wrep, e_ * S + q * QT, [[0, 128], [1, QT]]),
                      reads=kwrep_all, writes=[(f"{pre}wbc", 0)])
            for ic in range(32):
                buf = ic % 3
                kubw = (f"{pre}ucb", buf)
                base = (ic * 128) * 1024 if shared else ((e_ * 32 + ic) * 128) * 1024
                wh_u = w["sw_up"] if shared else w["up_w"]
                p.dma("sync", ucb[:, ts(buf, 1024)],
                      APx(wh_u, base, [[1024, 128], [1, 1024]]),
                      writes=[kubw])
                for sg in range(nsg):
                    pb, kpb = ctx.ps(2 + (ic * nsg + sg) % 2)
                    p.mm_acc(pb[:],
                             [(ucb[:, buf * 1024 + hc * 128: buf * 1024 + (hc + 1) * 128],
                               xqv(hc, sg)) for hc in range(NCH)],
                             reads=[kubw, (f"{pre}xq", 0)],
                             writes=[kpb])
                    hv = hact[:, ic * QT + sg * TG: ic * QT + (sg + 1) * TG]
                    p.op("scalar", lambda e, a=hv, b=pb, bi=ubT[:, ic:ic + 1]:
                         e.activation(a, b[:], AF.Gelu, bias=bi),
                         reads=[kpb, kub_bias], writes=[(f"{pre}hact", (ic, sg))])
            for hc in range(NCH):
                buf = hc % 2
                kdbw = (f"{pre}dcb", buf)
                base = (hc * 128) * 4096 if shared else ((e_ * 8 + hc) * 128) * 4096
                wh = w["sw_down"] if shared else w["down_w"]
                p.dma("scalar", dcb[:, ts(buf, 4096)],
                      APx(wh, base, [[4096, 128], [1, 4096]]),
                      writes=[(kdbw, 0)])
                kdb_both = [(kdbw, 0)]
                for sg in range(nsg):
                    pb, kpb = ctx.ps(4 + (hc * nsg + sg) % 2)
                    p.mm_acc(pb[:],
                             [(dcb[:, buf * 4096 + ic * 128: buf * 4096 + (ic + 1) * 128],
                               hact[:, ic * QT + sg * TG: ic * QT + (sg + 1) * TG]) for ic in range(32)],
                             reads=kdb_both + [(f"{pre}hact", (ic, sg)) for ic in range(32)],
                             writes=[kpb])
                    av = acc[:, hc * QT + sg * TG: hc * QT + (sg + 1) * TG]
                    kacc = (f"{pre}acc", (hc, sg))
                    bi = dbT[:, hc:hc + 1]
                    if shared:
                        p.op("vector", lambda e, a=av, b=pb, bb=bi:
                             e.tensor_scalar(a, b[:], bb, 0.1, op0=ALU.add, op1=ALU.mult),
                             reads=[kpb, kdb_bias], writes=[kacc])
                    else:
                        kyt = (f"{pre}ytmp", 0)
                        p.op("vector", lambda e, b=pb, bb=bi:
                             e.tensor_scalar_add(ytmp[:], b[:], bb),
                             reads=[kpb, kdb_bias], writes=[kyt])
                        p.op("vector", lambda e, wv=wbc[:, ts(sg, TG)]: e.tensor_mul(ytmp[:], ytmp[:], wv),
                             reads=[kyt, (f"{pre}wbc", 0)], writes=[kyt])
                        p.op("vector", lambda e, a=av: e.tensor_add(a, a, ytmp[:]),
                             reads=[kyt, kacc], writes=[kacc])
        if variant == "L1":
            for hc in range(NCH):
                for sg in range(nsg):
                    av = acc[:, hc * QT + sg * TG: hc * QT + (sg + 1) * TG]
                    emit_residual_store(ctx, pre, 0, av, (f"{pre}acc", (hc, sg)),
                                        xin_ap, xout_ap, hc, q * QT + sg * TG, TG,
                                        (epx, epy), xin_key=xin_key)
        else:
            ptT, kptT = ctx.ps(0)
            ptB, kptB = ctx.ps(1)
            for tt in range(QT // 128):
                sg = tt * 128 // TG
                for hc in range(NCH):
                    av = acc[:, hc * QT + tt * 128: hc * QT + (tt + 1) * 128]
                    p.transpose(ptT[0:128, 0:128], av, idf[:],
                                reads=[(f"{pre}acc", (hc, sg)), ctx.KID], writes=[kptT])
                    p.op("vector", lambda e, a=tmT[:, ts(hc)]: e.tensor_copy(a, ptT[0:128, 0:128]),
                         reads=[kptT], writes=[(f"{pre}tmT", hc)])
                m, r = mx[:, 0:1], mx[:, 1:2]
                p.op("vector", lambda e: e.reduce_max(m, tmT[:], axis=AX.X, apply_absolute_value=True),
                     reads=[(f"{pre}tmT", h) for h in range(NCH)], writes=[(f"{pre}mx", 0)])
                p.op("vector", lambda e: e.tensor_scalar_add(m, m, 1e-6),
                     reads=[(f"{pre}mx", 0)], writes=[(f"{pre}mx", 0)])
                p.op("vector", lambda e: e.reciprocal(r, m),
                     reads=[(f"{pre}mx", 0)], writes=[(f"{pre}mx", 1)])
                for hc in range(NCH):
                    p.op("vector", lambda e, a=tmT[:, ts(hc)]: e.tensor_scalar_mul(a, a, r),
                         reads=[(f"{pre}tmT", hc), (f"{pre}mx", 1)], writes=[(f"{pre}tmT", hc)])
                    p.transpose(ptB[0:128, 0:128], tmT[:, ts(hc)], idf[:],
                                reads=[(f"{pre}tmT", hc), ctx.KID], writes=[kptB])
                    emit_residual_store(ctx, pre, 0, ptB[0:128, 0:128], kptB,
                                        xin_ap, xout_ap, hc, q * QT + tt * 128, 128,
                                        (epx, epy), xin_key=xin_key)



def emit_proj_residual(ctx, pre, wdram, w_base, bT, src_d, src_keys, xin_ap, xout_ap,
                       stack, xin_key="XIN", kin=1024):
    nc, p = ctx.nc, ctx.p
    nkc = kin // 128
    wb = stack.enter_context(nc.sbuf_tensor(f"{pre}_pw_b", [128, nkc * 1024], BF16))
    sb = stack.enter_context(nc.sbuf_tensor(f"{pre}_psrc", [128, nkc * TG], BF16))
    yt = stack.enter_context(nc.sbuf_tensor(f"{pre}_py", [128, TG], F32))
    epx = stack.enter_context(nc.sbuf_tensor(f"{pre}_pex", [128, TG], F32))
    epy = stack.enter_context(nc.sbuf_tensor(f"{pre}_pey", [128, TG], F32))
    for kc in range(nkc):
        kwb = (f"{pre}pwb", kc)
        p.dma("sync", wb[:, kc * 1024:(kc + 1) * 1024],
              APx(wdram, w_base + kc * 128 * 1024, [[1024, 128], [1, 1024]]),
              writes=[kwb])
    kwb_all = [(f"{pre}pwb", kc) for kc in range(nkc)]
    for tg in range(NTG):
        ksrc = (f"{pre}psrc", 0)
        p.dma("sync", sb[:].rearrange("p (c t) -> p c t", t=TG),
              APx(src_d, tg * TG, [[S, 128], [128 * S, nkc], [1, TG]]),
              reads=src_keys, writes=[ksrc])
        for oc in range(NCH):
            pb, kpb = ctx.ps(2 + oc % 2)
            p.mm_acc(pb[:],
                     [(wb[:, kc * 1024 + oc * 128: kc * 1024 + (oc + 1) * 128],
                       sb[:, ts(kc, TG)]) for kc in range(nkc)],
                     reads=kwb_all + [ksrc], writes=[kpb])
            kyt = (f"{pre}py", 0)
            if bT is not None:
                bTt, kb = bT
                p.op("vector", lambda e, b=pb, bb=bTt[:, oc:oc + 1]:
                     e.tensor_scalar_add(yt[:], b[:], bb),
                     reads=[kpb, kb], writes=[kyt])
            else:
                p.op("vector", lambda e, b=pb: e.tensor_copy(yt[:], b[:]),
                     reads=[kpb], writes=[kyt])
            emit_residual_store(ctx, pre + "P", 0, yt[:], kyt, xin_ap, xout_ap,
                                oc, tg * TG, TG, (epx, epy), xin_key=xin_key)



def l1b_arrs(inp):
    k = inp["L1_attn_b_kernel"]
    wq = np.ascontiguousarray(k[0].transpose(1, 0, 2).reshape(1024, 1024))
    wk = np.ascontiguousarray(k[1].transpose(1, 0, 2).reshape(1024, 1024))
    wv = np.ascontiguousarray(k[2].transpose(1, 0, 2).reshape(1024, 1024))
    wo = inp["L1_attn_b_WO"][0]
    d_idx, h_idx = np.meshgrid(np.arange(64), np.arange(16), indexing="ij")
    perm = (d_idx * 16 + h_idx)
    wo2 = np.zeros_like(wo)
    for h in range(16):
        for d in range(64):
            wo2[h * 64 + d] = wo[d * 16 + h]
    return dict(wq=wq, wk=wk, wv=wv, wo=np.ascontiguousarray(wo2))


def emit_attn_b1(ctx, xin_ap, xout_ap, w, stack, pre, xin_key="XIN"):
    import os
    NPAIR = int(os.environ.get("B1PAIRS", "8"))
    NQT = int(os.environ.get("B1QT", "32"))
    nc, p = ctx.nc, ctx.p
    SC = 1.0 / (64 * 64 * 16)

    xnd = nc.dram_tensor(f"{pre}_xnd", [D * S], BF16, kind="Internal")
    emit_ln_prologue(ctx, xin_ap, stack, pre + "L", xin_key=xin_key, spill=xnd)
    ctx_d = nc.dram_tensor(f"{pre}_ctxd", [D * S], BF16, kind="Internal")

    xnb = stack.enter_context(nc.sbuf_tensor(f"{pre}_xnb", [128, NCH * TG], BF16))
    wpb = stack.enter_context(nc.sbuf_tensor(f"{pre}_wpb", [128, 3 * 8 * 128], BF16))
    qp = stack.enter_context(nc.sbuf_tensor(f"{pre}_qp", [128, S], BF16))
    kp = stack.enter_context(nc.sbuf_tensor(f"{pre}_kp", [128, S], BF16))
    vp = stack.enter_context(nc.sbuf_tensor(f"{pre}_vp", [128, S], BF16))
    vT = stack.enter_context(nc.sbuf_tensor(f"{pre}_vT", [128, S], BF16))
    pb_ = stack.enter_context(nc.sbuf_tensor(f"{pre}_pb", [128, 4 * S], BF16))
    ls_ = stack.enter_context(nc.sbuf_tensor(f"{pre}_ls", [128, 16], F32))
    pT = stack.enter_context(nc.sbuf_tensor(f"{pre}_pT", [128, S], BF16))
    mm_ = stack.enter_context(nc.sbuf_tensor(f"{pre}_mm", [128, 8], F32))
    cxe = stack.enter_context(nc.sbuf_tensor(f"{pre}_cxe", [64, 256], BF16))

    for pr in range(NPAIR):
        for wi, wn in enumerate(("wq", "wk", "wv")):
            p.dma("sync", wpb[:, ts(wi, 1024)].rearrange("p (c f) -> p c f", f=128),
                  APx(w[wn], pr * 128, [[1024, 128], [128 * 1024, 8], [1, 128]]),
                  writes=[(f"{pre}wpb", wi)])
        for tg in range(NTG):
            kx = (f"{pre}xnb", 0)
            p.dma("sync", xnb[:].rearrange("p (c t) -> p c t", t=TG),
                  APx(xnd, tg * TG, [[S, 128], [128 * S, NCH], [1, TG]]),
                  reads=[("XND", (tg, c_)) for c_ in range(NCH)], writes=[kx])
            for wi, dst, kd in ((0, qp, "qp"), (1, kp, "kp"), (2, vp, "vp")):
                pbk, kpb = ctx.ps(2 + wi % 2)
                p.mm_acc(pbk[:],
                         [(wpb[:, wi * 1024 + c * 128: wi * 1024 + (c + 1) * 128],
                           xnb[:, ts(c, TG)]) for c in range(NCH)],
                         reads=[(f"{pre}wpb", wi), kx], writes=[kpb])
                p.op("vector", lambda e, a=dst[:, ts(tg, TG)], b=pbk: e.tensor_copy(a, b[:]),
                     reads=[kpb], writes=[(f"{pre}{kd}", tg)])
        kq_all = [(f"{pre}qp", t) for t in range(NTG)]
        kk_all = [(f"{pre}kp", t) for t in range(NTG)]
        for g in range(4):
            ctx.batch_transpose(p, [vp[:, ts(8 * g + i)] for i in range(8)],
                                [(f"{pre}vp", t_) for t_ in range(NTG)],
                                vT[:, g * 1024:(g + 1) * 1024], (f"{pre}vT", g), g)
        kvT_all = [(f"{pre}vT", g) for g in range(4)]
        NIT = 2 * NQT

        def stageA(hq):
            par = hq % 2
            par4 = hq % 4
            hb = (hq // NQT) * 64
            qt = hq % NQT
            pv = pb_[:, par4 * S:(par4 + 1) * S]
            lsv = ls_[:, par * 8:(par + 1) * 8]
            l_, rl = mm_[:, par * 4 + 2: par * 4 + 3], mm_[:, par * 4 + 3: par * 4 + 4]
            for jg in range(NTG):
                pbs, kpbs = ctx.ps(2 + jg % 3)
                p.mm(pbs[:], qp[hb:hb + 64, ts(qt)], kp[hb:hb + 64, ts(jg, TG)],
                     start=True, stop=True,
                     reads=kq_all + kk_all, writes=[kpbs])
                p.op("scalar", lambda e, a=pv[:, ts(jg, TG)], b=pbs,
                     c=lsv[:, jg:jg + 1]: e.activation(a, b[:], AF.Exp,
                                                       scale=SC, accum_out=c),
                     reads=[kpbs], writes=[(f"{pre}pb", par4), (f"{pre}ls", (par, jg))])
            p.op("vector", lambda e, a=l_, s=lsv: e.reduce_sum(a, s, axis=AX.X),
                 reads=[(f"{pre}ls", (par, j)) for j in range(NTG)],
                 writes=[(f"{pre}m", (par, 2))])
            p.op("vector", lambda e, a=rl, b=l_: e.reciprocal(a, b),
                 reads=[(f"{pre}m", (par, 2))], writes=[(f"{pre}m", (par, 3))])
            p.op("vector", lambda e, a=pv, b=rl: e.tensor_scalar_mul(a, a, b),
                 reads=[(f"{pre}pb", par4), (f"{pre}m", (par, 3))], writes=[(f"{pre}pb", par4)])

        def stageB(hq):
            par = hq % 2
            par4 = hq % 4
            hb = (hq // NQT) * 64
            qt = hq % NQT
            pv = pb_[:, par4 * S:(par4 + 1) * S]
            cx = cxe[:, par * 128:(par + 1) * 128]
            for g in range(4):
                ctx.batch_transpose(p, [pv[:, ts(8 * g + i)] for i in range(8)],
                                    [(f"{pre}pb", par4)],
                                    pT[:, g * 1024:(g + 1) * 1024], (f"{pre}pT", g), g)
            pbc, kpbc = ctx.ps(5)
            p.mm_acc(pbc[0:64, 0:128],
                     [(vT[:, j * 128 + hb: j * 128 + hb + 64], pT[:, ts(j)])
                      for j in range(32)],
                     reads=kvT_all + [(f"{pre}pT", g) for g in range(4)],
                     writes=[kpbc])
            p.op("vector", lambda e, a=cx, b=pbc: e.tensor_copy(a, b[0:64, 0:128]),
                 reads=[kpbc], writes=[(f"{pre}cxe", par)])
            p.dma("sync", APx(ctx_d, (pr * 128 + hb) * S + qt * 128, [[S, 64], [1, 128]]),
                  cx, reads=[(f"{pre}cxe", par)], writes=[("CTXD", (pr, hb, qt))])

        for hq in range(NIT + 2):
            if hq >= 2:
                stageB(hq - 2)
            if hq < NIT:
                stageA(hq)

    emit_proj_residual(ctx, pre, w["wo"], 0, None, ctx_d,
                       [("CTXD", (pr_, hb_, qt_)) for pr_ in range(8)
                        for hb_ in (0, 64) for qt_ in range(NQT)],
                       xin_ap, xout_ap, stack, xin_key=xin_key)



def l0b_arrs(inp):
    return dict(wq=inp["L0_attn_b_Wq"], wm=inp["L0_attn_b_Wm"],
                w1=inp["L0_attn_b_W1"], w2=inp["L0_attn_b_W2"],
                g1=inp["L0_attn_b_g1"], b1=inp["L0_attn_b_b1"],
                g2=inp["L0_attn_b_g2"], b2=inp["L0_attn_b_b2"])


def emit_ln_gb_residual(ctx, pre, src_fp32, src_b, src_keys, gT, bT, base_ap,
                        base_keys_fn, xout_ap, tg, bufs, stack_bufs, src_keys_b=None,
                        extra=None):
    
    nc, p = ctx.nc, ctx.p
    sqt, stat, epsc = stack_bufs
    ps0, kps0 = ctx.ps(0)
    ps1, kps1 = ctx.ps(1)
    ones = ctx.ones_bf
    gTt, kg = gT
    bTt, kb = bT
    if src_keys_b is None:
        src_keys_b = src_keys
    mu, msq = stat[:, ts(0, TG)], stat[:, ts(1, TG)]
    var, rr = stat[:, ts(1, TG)], stat[:, ts(2, TG)]
    for c in range(NCH):
        sbv = src_b[:, ts(c, TG)]
        sqv = sqt[:]
        ksq = (f"{pre}lsq", 0)
        p.op("gpsimd", lambda e, a=sqv, b=sbv: e.tensor_mul(a, b, b),
             reads=[src_keys_b(c)], writes=[ksq])
        p.mm(ps0[:], ones[:], sbv, start=(c == 0), stop=(c == NCH - 1),
             reads=[src_keys_b(c), ctx.KONE], writes=[kps0] if c in (0, NCH - 1) else ())
        p.mm(ps1[:], ones[:], sqv, start=(c == 0), stop=(c == NCH - 1),
             reads=[ksq, ctx.KONE], writes=[kps1] if c in (0, NCH - 1) else ())
    kmu, kvar, krr = (f"{pre}lmu", 0), (f"{pre}lvar", 0), (f"{pre}lrr", 0)
    p.op("vector", lambda e: e.tensor_scalar_mul(mu, ps0[:], 1.0 / D),
         reads=[kps0], writes=[kmu])
    p.op("vector", lambda e: e.tensor_mul(msq, mu, mu),
         reads=[kmu], writes=[(f"{pre}lmsq", 0)])
    p.op("vector", lambda e: e.scalar_tensor_tensor(var, ps1[:], 1.0 / D, msq,
                                                    op0=ALU.mult, op1=ALU.subtract),
         reads=[kps1, (f"{pre}lmsq", 0)], writes=[kvar])
    p.op("scalar", lambda e: e.activation(rr, var, AF.Ln, bias=epsc[:]),
         reads=[kvar, (f"{pre}leps", "*")], writes=[krr])
    p.op("scalar", lambda e: e.activation(rr, rr, AF.Exp, scale=-0.5),
         reads=[krr], writes=[krr])
    for c in range(NCH):
        sfv = src_fp32[:, ts(c, TG)]
        kyt = (f"{pre}lyt", 0)
        p.op("vector", lambda e, a=sfv: e.tensor_sub(a, a, mu),
             reads=[src_keys(c), kmu], writes=[src_keys(c)])
        p.op("vector", lambda e, a=sfv: e.tensor_mul(a, a, rr),
             reads=[src_keys(c), krr], writes=[src_keys(c)])
        p.op("vector", lambda e, a=sfv, g=gTt[:, c:c + 1], bb=bTt[:, c:c + 1]:
             e.tensor_scalar(a, a, g, bb, op0=ALU.mult, op1=ALU.add),
             reads=[src_keys(c), kg, kb], writes=[src_keys(c)])
        if extra is not None:
            xdram, xkeys_fn, xbuf = extra
            kxe = (f"{pre}lxe", 0)
            p.dma("scalar", xbuf[:], APx(xdram, (c * 128) * S + tg * TG, [[S, 128], [1, TG]]),
                  reads=xkeys_fn(c, tg), writes=[kxe])
            p.op("vector", lambda e, a=sfv: e.tensor_add(a, a, xbuf[:]),
                 reads=[src_keys(c), kxe], writes=[src_keys(c)])
        emit_residual_store(ctx, pre + "E", 0, sfv, src_keys(c), base_ap, xout_ap,
                            c, tg * TG, TG, bufs, xin_keys=base_keys_fn(c, tg))


def emit_attn_b0(ctx, xin_ap, xout_ap, w, stack, pre, xin_key="XIN"):
    nc, p = ctx.nc, ctx.p
    L = float(S)

    xnd = nc.dram_tensor(f"{pre}_xnd", [D * S], BF16, kind="Internal")
    msg_d = nc.dram_tensor(f"{pre}_msgd", [D * S], BF16, kind="Internal")
    m2_d = nc.dram_tensor(f"{pre}_m2d", [D * S], BF16, kind="Internal")
    zd = nc.dram_tensor(f"{pre}_zd", [S], F32, kind="Internal")

    g1T = load_vecT(ctx, f"{pre}g1T", w["g1"], 0, 1024, stack)
    b1T = load_vecT(ctx, f"{pre}b1T", w["b1"], 0, 1024, stack)
    g2T = load_vecT(ctx, f"{pre}g2T", w["g2"], 0, 1024, stack)
    b2T = load_vecT(ctx, f"{pre}b2T", w["b2"], 0, 1024, stack)
    epx = stack.enter_context(nc.sbuf_tensor(f"{pre}_epx", [128, TG], F32))
    epy = stack.enter_context(nc.sbuf_tensor(f"{pre}_epy", [128, TG], F32))

    st1 = ExitStack()
    emit_ln_prologue(ctx, xin_ap, st1, pre + "L", xin_key=xin_key, spill=xnd)
    stack_outer, stack = stack, st1
    xnb = stack.enter_context(nc.sbuf_tensor(f"{pre}_xnb", [128, NCH * TG], BF16))
    wqb = stack.enter_context(nc.sbuf_tensor(f"{pre}_wqb", [128, 1024], BF16))
    qsb = stack.enter_context(nc.sbuf_tensor(f"{pre}_qsb", [128, S], F32))
    fb = stack.enter_context(nc.sbuf_tensor(f"{pre}_fb", [128, S], BF16))
    vb = stack.enter_context(nc.sbuf_tensor(f"{pre}_vb", [128, S], BF16))
    fT = stack.enter_context(nc.sbuf_tensor(f"{pre}_fT", [128, S], BF16))
    vT = stack.enter_context(nc.sbuf_tensor(f"{pre}_vT", [128, S], BF16))
    scr = stack.enter_context(nc.sbuf_tensor(f"{pre}_scr", [128, TG], F32))
    ks = stack.enter_context(nc.sbuf_tensor(f"{pre}_ks", [128, NTG + 2], F32))
    kvb = stack.enter_context(nc.sbuf_tensor(f"{pre}_kvb", [128, 128], BF16))
    zrow = stack.enter_context(nc.sbuf_tensor(f"{pre}_zrow", [1, S], F32))
    zrep = stack.enter_context(nc.sbuf_tensor(f"{pre}_zrep", [128, TG], F32))
    msgb = stack.enter_context(nc.sbuf_tensor(f"{pre}_msgb", [128, TG], BF16))

    for h in range(NCH):
        kwq = (f"{pre}wq", 0)
        p.dma("sync", wqb[:].rearrange("p (c f) -> p c f", f=128),
              APx(w["wq"], h * 128, [[1024, 128], [128 * 1024, 8], [1, 128]]),
              writes=[kwq])
        for tg in range(NTG):
            kx = (f"{pre}xnb", 0)
            p.dma("sync", xnb[:].rearrange("p (c t) -> p c t", t=TG),
                  APx(xnd, tg * TG, [[S, 128], [128 * S, NCH], [1, TG]]),
                  reads=[("XND", (tg, c_)) for c_ in range(NCH)], writes=[kx])
            pbk, kpb = ctx.ps(2 + tg % 2)
            p.mm_acc(pbk[:], [(wqb[:, ts(c)], xnb[:, ts(c, TG)]) for c in range(NCH)],
                     reads=[kwq, kx], writes=[kpb])
            kq = (f"{pre}q", tg)
            p.op("vector", lambda e, a=qsb[:, ts(tg, TG)], b=pbk: e.tensor_copy(a, b[:]),
                 reads=[kpb], writes=[kq])
            qv = qsb[:, ts(tg, TG)]
            kscr = (f"{pre}scr", 0)
            p.op("vector", lambda e, q_=qv: e.tensor_scalar_min(scr[:], q_, 0.0),
                 reads=[kq], writes=[kscr])
            p.op("scalar", lambda e: e.activation(scr[:], scr[:], AF.Exp),
                 reads=[kscr], writes=[kscr])
            kf = (f"{pre}fb", tg)
            p.op("vector", lambda e, a=fb[:, ts(tg, TG)], q_=qv:
                 e.scalar_tensor_tensor(a, q_, 0.0, scr[:], op0=ALU.max, op1=ALU.add),
                 reads=[kq, kscr], writes=[kf])
            p.op("vector", lambda e, a=vb[:, ts(tg, TG)], q_=qv: e.tensor_scalar_mul(a, q_, 1.0 / L),
                 reads=[kq], writes=[(f"{pre}vb", tg)])
            p.op("vector", lambda e, a=ks[:, tg:tg + 1], b=fb[:, ts(tg, TG)]:
                 e.reduce_sum(a, b, axis=AX.X),
                 reads=[kf], writes=[(f"{pre}ks", tg)])
        p.op("vector", lambda e: e.reduce_sum(ks[:, NTG:NTG + 1], ks[:, 0:NTG], axis=AX.X),
             reads=[(f"{pre}ks", t_) for t_ in range(NTG)], writes=[(f"{pre}ks", "sum")])
        ksb = stack.enter_context(nc.sbuf_tensor(f"{pre}_ksb{h}", [128, 1], BF16))
        p.op("vector", lambda e, a=ksb[:]: e.tensor_copy(a, ks[:, NTG:NTG + 1]),
             reads=[(f"{pre}ks", "sum")], writes=[(f"{pre}ksb", 0)])
        for tg in range(NTG):
            pz, kpz = ctx.ps(4)
            p.mm(pz[0:1, :], ksb[:], fb[:, ts(tg, TG)], start=True, stop=True,
                 reads=[(f"{pre}ksb", 0), (f"{pre}fb", tg)], writes=[kpz])
            kz = (f"{pre}zrow", tg)
            p.op("vector", lambda e, a=zrow[0:1, ts(tg, TG)], b=pz:
                 e.tensor_scalar_add(a, b[0:1, :], 1e-6),
                 reads=[kpz], writes=[kz])
            p.op("vector", lambda e, a=zrow[0:1, ts(tg, TG)]: e.reciprocal(a, a),
                 reads=[kz], writes=[kz])
            p.op("vector", lambda e, a=zrow[0:1, ts(tg, TG)]: e.tensor_scalar_mul(a, a, L),
                 reads=[kz], writes=[kz])
        p.dma("sync", APx(zd, 0, [[1, 1], [1, S]]), zrow[0:1, :],
              reads=[(f"{pre}zrow", t_) for t_ in range(NTG)], writes=[(f"{pre}zd", "*")])
        for g in range(4):
            ctx.batch_transpose(p, [fb[:, ts(8 * g + i)] for i in range(8)],
                                [(f"{pre}fb", t_) for t_ in range(NTG)],
                                fT[:, g * 1024:(g + 1) * 1024], (f"{pre}fT", g), g)
            ctx.batch_transpose(p, [vb[:, ts(8 * g + i)] for i in range(8)],
                                [(f"{pre}vb", t_) for t_ in range(NTG)],
                                vT[:, g * 1024:(g + 1) * 1024], (f"{pre}vT", g), g + 1)
        pkv, kpkv = ctx.ps(5)
        p.mm_acc(pkv[0:128, 0:128],
                 [(fT[:, ts(j)], vT[:, ts(j)]) for j in range(32)],
                 reads=[(f"{pre}fT", g) for g in range(4)] + [(f"{pre}vT", g) for g in range(4)],
                 writes=[kpkv])
        p.op("vector", lambda e: e.tensor_copy(kvb[:], pkv[0:128, 0:128]),
             reads=[kpkv], writes=[(f"{pre}kvb", 0)])
        for tg in range(NTG):
            kzr = (f"{pre}zrep", 0)
            p.dma("gpsimd", zrep[:], APx(zd, tg * TG, [[0, 128], [1, TG]]),
                  reads=[(f"{pre}zd", "*")], writes=[kzr])
            pm, kpm = ctx.ps(2 + tg % 2)
            p.mm(pm[:], kvb[:], fb[:, ts(tg, TG)], start=True, stop=True,
                 reads=[(f"{pre}kvb", 0), (f"{pre}fb", tg)], writes=[kpm])
            kms = (f"{pre}msg", 0)
            p.op("vector", lambda e, a=scr[:], b=pm: e.tensor_mul(a, b[:], zrep[:]),
                 reads=[kpm, kzr], writes=[kms])
            kmb = (f"{pre}msgb", 0)
            p.op("vector", lambda e: e.tensor_copy(msgb[:], scr[:]),
                 reads=[kms], writes=[kmb])
            p.dma("scalar", APx(msg_d, h * 128 * S + tg * TG, [[S, 128], [1, TG]]),
                  msgb[:], reads=[kmb], writes=[("MSGD", (h, tg))])

    p.barrier()
    st1.close()
    stack = stack_outer
    wmb = stack.enter_context(nc.sbuf_tensor(f"{pre}_wmb", [128, NCH * 1024], BF16))
    msb = stack.enter_context(nc.sbuf_tensor(f"{pre}_msb", [128, NCH * TG], BF16))
    m2f = stack.enter_context(nc.sbuf_tensor(f"{pre}_m2f", [128, NCH * TG], F32))
    m2b = stack.enter_context(nc.sbuf_tensor(f"{pre}_m2b", [128, NCH * TG], BF16))
    sqt = stack.enter_context(nc.sbuf_tensor(f"{pre}_sqt", [128, TG], BF16))
    stat2 = stack.enter_context(nc.sbuf_tensor(f"{pre}_st2", [128, 3 * TG], F32))
    eps2 = stack.enter_context(nc.sbuf_tensor(f"{pre}_eps2", [128, 1], F32))
    xtb = stack.enter_context(nc.sbuf_tensor(f"{pre}_xtb", [128, TG], BF16))
    p.op("vector", lambda e: e.memset(eps2[:], EPS), writes=[(f"{pre}leps", "*")])
    for kc in range(NCH):
        p.dma("sync", wmb[:, ts(kc, 1024)],
              APx(w["wm"], kc * 128 * 1024, [[1024, 128], [1, 1024]]),
              writes=[(f"{pre}wmb", kc)])
    kwm_all = [(f"{pre}wmb", kc) for kc in range(NCH)]
    for tg in range(NTG):
        km = (f"{pre}msb", 0)
        p.dma("sync", msb[:].rearrange("p (c t) -> p c t", t=TG),
              APx(msg_d, tg * TG, [[S, 128], [128 * S, NCH], [1, TG]]),
              reads=[("MSGD", (h_, tg)) for h_ in range(NCH)], writes=[km])
        for oc in range(NCH):
            pb2, kpb2 = ctx.ps(2 + oc % 2)
            p.mm_acc(pb2[:],
                     [(wmb[:, kc * 1024 + oc * 128: kc * 1024 + (oc + 1) * 128],
                       msb[:, ts(kc, TG)]) for kc in range(NCH)],
                     reads=kwm_all + [km], writes=[kpb2])
            km2 = lambda c: (f"{pre}m2f", c)
            p.op("vector", lambda e, a=m2f[:, ts(oc, TG)], b=pb2: e.tensor_copy(a, b[:]),
                 reads=[kpb2], writes=[km2(oc)])
            p.op("vector", lambda e, a=m2b[:, ts(oc, TG)], b=m2f[:, ts(oc, TG)]: e.tensor_copy(a, b),
                 reads=[km2(oc)], writes=[(f"{pre}m2b", oc)])
            p.dma("scalar", APx(m2_d, oc * 128 * S + tg * TG, [[S, 128], [1, TG]]),
                  m2b[:, ts(oc, TG)], reads=[(f"{pre}m2b", oc)], writes=[("M2D", (oc, tg))])
        emit_ln_gb_residual(ctx, pre + "1", m2f, m2b,
                            lambda c: (f"{pre}m2f", c), g1T, b1T,
                            xin_ap, lambda c, t_: [(xin_key, "*")], xout_ap, tg,
                            (epx, epy), (sqt, stat2, eps2),
                            src_keys_b=lambda c: (f"{pre}m2b", c),
                            extra=(xnd, lambda c, t_: [("XND", (t_, c))], xtb))

    w1b = stack.enter_context(nc.sbuf_tensor(f"{pre}_w1b", [128, NCH * 2048], BF16))
    w2b = stack.enter_context(nc.sbuf_tensor(f"{pre}_w2b", [128, 16 * 1024], BF16))
    h1b = stack.enter_context(nc.sbuf_tensor(f"{pre}_h1b", [128, 16 * TG], BF16))
    for kc in range(NCH):
        p.dma("sync", w1b[:, ts(kc, 2048)],
              APx(w["w1"], kc * 128 * 2048, [[2048, 128], [1, 2048]]),
              writes=[(f"{pre}w1b", kc)])
    for kc in range(16):
        p.dma("sync", w2b[:, ts(kc, 1024)],
              APx(w["w2"], kc * 128 * 1024, [[1024, 128], [1, 1024]]),
              writes=[(f"{pre}w2b", kc)])
    kw1_all = [(f"{pre}w1b", kc) for kc in range(NCH)]
    kw2_all = [(f"{pre}w2b", kc) for kc in range(16)]
    for tg in range(NTG):
        km = (f"{pre}msb", 0)
        p.dma("sync", msb[:].rearrange("p (c t) -> p c t", t=TG),
              APx(m2_d, tg * TG, [[S, 128], [128 * S, NCH], [1, TG]]),
              reads=[("M2D", (c_, tg)) for c_ in range(NCH)], writes=[km])
        for o2 in range(16):
            pb3, kpb3 = ctx.ps(2 + o2 % 2)
            p.mm_acc(pb3[:],
                     [(w1b[:, kc * 2048 + o2 * 128: kc * 2048 + (o2 + 1) * 128],
                       msb[:, ts(kc, TG)]) for kc in range(NCH)],
                     reads=kw1_all + [km], writes=[kpb3])
            p.op("scalar", lambda e, a=h1b[:, ts(o2, TG)], b=pb3:
                 e.activation(a, b[:], AF.Relu),
                 reads=[kpb3], writes=[(f"{pre}h1b", o2)])
        kh1_all = [(f"{pre}h1b", o_) for o_ in range(16)]
        for oc in range(NCH):
            pb4, kpb4 = ctx.ps(4 + oc % 2)
            p.mm_acc(pb4[:],
                     [(w2b[:, kc * 1024 + oc * 128: kc * 1024 + (oc + 1) * 128],
                       h1b[:, ts(kc, TG)]) for kc in range(16)],
                     reads=kw2_all + kh1_all, writes=[kpb4])
            p.op("vector", lambda e, a=m2f[:, ts(oc, TG)], b=pb4: e.tensor_copy(a, b[:]),
                 reads=[kpb4], writes=[(f"{pre}m2f", oc)])
            p.op("vector", lambda e, a=m2b[:, ts(oc, TG)], b=m2f[:, ts(oc, TG)]: e.tensor_copy(a, b),
                 reads=[(f"{pre}m2f", oc)], writes=[(f"{pre}m2b", oc)])
        emit_ln_gb_residual(ctx, pre + "2", m2f, m2b,
                            lambda c: (f"{pre}m2f", c), g2T, b2T,
                            xout_ap, lambda c, t_: [("XOUT", (c, t_ * TG))], xout_ap, tg,
                            (epx, epy), (sqt, stat2, eps2),
                            src_keys_b=lambda c: (f"{pre}m2b", c))



def l1a_arrs(inp):
    return dict(wqkv=inp["L1_attn_a_Wqkv"], bqkv=inp["L1_attn_a_bqkv"],
                wln=inp["L1_attn_a_Wln"], bln=inp["L1_attn_a_bln"])


def emit_attn_a1(ctx, xin_ap, xout_ap, w, stack, pre, xin_key="XIN"):
    import os
    PH = os.environ.get("A1PH", "C")
    nc, p = ctx.nc, ctx.p
    EPS_SUM, EPS_NORM = 1e-7, 1e-12

    xnd = nc.dram_tensor(f"{pre}_xnd", [D * S], BF16, kind="Internal")
    q2_d = nc.dram_tensor(f"{pre}_q2d", [D * S], BF16, kind="Internal")
    k2_d = nc.dram_tensor(f"{pre}_k2d", [D * S], BF16, kind="Internal")
    k2f_d = nc.dram_tensor(f"{pre}_k2fd", [D * S], BF16, kind="Internal")
    v_d = nc.dram_tensor(f"{pre}_vd", [D * S], BF16, kind="Internal")

    blnT = load_vecT(ctx, f"{pre}blnT", w["bln"], 0, 1024, stack)
    bqT = load_vecT(ctx, f"{pre}bqT", w["bqkv"], 0, 1024, stack)
    bkT = load_vecT(ctx, f"{pre}bkT", w["bqkv"], 1024, 1024, stack)
    bvT = load_vecT(ctx, f"{pre}bvT", w["bqkv"], 2048, 1024, stack)
    epx = stack.enter_context(nc.sbuf_tensor(f"{pre}_epx", [128, TG], F32))
    epy = stack.enter_context(nc.sbuf_tensor(f"{pre}_epy", [128, TG], F32))
    kv_sb = stack.enter_context(nc.sbuf_tensor(f"{pre}_kvsb", [128, NCH * 1024], F32))
    kv_b = stack.enter_context(nc.sbuf_tensor(f"{pre}_kvb", [128, NCH * 1024], BF16))

    stA = ExitStack()
    emit_ln_prologue(ctx, xin_ap, stA, pre + "L", xin_key=xin_key, spill=xnd)
    wqkvb = stA.enter_context(nc.sbuf_tensor(f"{pre}_wqkvb", [128, 24 * 1024], BF16))
    xnb = stA.enter_context(nc.sbuf_tensor(f"{pre}_xnb", [128, NCH * TG], BF16))
    qtg = stA.enter_context(nc.sbuf_tensor(f"{pre}_qtg", [128, NCH * TG], F32))
    sqb = stA.enter_context(nc.sbuf_tensor(f"{pre}_sqb", [128, NCH * TG], BF16))
    stt_ = stA.enter_context(nc.sbuf_tensor(f"{pre}_stt", [128, 3 * TG], F32))
    nacc = stA.enter_context(nc.sbuf_tensor(f"{pre}_nacc", [128, 12], F32))
    outb = stA.enter_context(nc.sbuf_tensor(f"{pre}_outb", [128, TG], BF16))
    for cb in range(24):
        p.dma("sync", wqkvb[:, ts(cb, 1024)].rearrange("p (c f) -> p c f", f=128),
              APx(w["wqkv"], cb * 128, [[3072, 128], [128 * 3072, 8], [1, 128]]),
              writes=[(f"{pre}wqkvb", cb)])
    p.op("vector", lambda e: e.memset(nacc[:, 0:8], 0.0), writes=[(f"{pre}nacc", "*")])
    ps0, kps0 = ctx.ps(0)
    ones = ctx.ones_bf
    s2r, r2r, rlr = stt_[:, ts(0, TG)], stt_[:, ts(1, TG)], stt_[:, ts(2, TG)]

    def ones_sum(src_tiles_key_fn, srcbuf, dst, kdst):
        for c in range(NCH):
            p.mm(ps0[:], ones[:], srcbuf[:, ts(c, TG)], start=(c == 0), stop=(c == NCH - 1),
                 reads=[src_tiles_key_fn(c), ctx.KONE],
                 writes=[kps0] if c in (0, NCH - 1) else ())
        p.op("vector", lambda e: e.tensor_copy(dst, ps0[:]), reads=[kps0], writes=[kdst])

    for tg in range(NTG):
        kx = (f"{pre}xnb", 0)
        p.dma("sync", xnb[:].rearrange("p (c t) -> p c t", t=TG),
              APx(xnd, tg * TG, [[S, 128], [128 * S, NCH], [1, TG]]),
              reads=[("XND", (tg, c_)) for c_ in range(NCH)], writes=[kx])
        for part, bT, outd in ((0, bqT, q2_d), (1, bkT, k2_d), (2, bvT, v_d)):
            bTt, kbT = bT
            for oc in range(NCH):
                cb = part * 8 + oc
                pb, kpb = ctx.ps(2 + oc % 2)
                p.mm_acc(pb[:],
                         [(wqkvb[:, cb * 1024 + kc * 128: cb * 1024 + (kc + 1) * 128],
                           xnb[:, ts(kc, TG)]) for kc in range(NCH)],
                         reads=[(f"{pre}wqkvb", cb), kx], writes=[kpb])
                kq = (f"{pre}qtg", oc)
                p.op("vector", lambda e, a=qtg[:, ts(oc, TG)], b=pb, bb=bTt[:, oc:oc + 1]:
                     e.tensor_scalar_add(a, b[:], bb),
                     reads=[kpb, kbT], writes=[kq])
            if part == 2:
                for oc in range(NCH):
                    kq, ko = (f"{pre}qtg", oc), (f"{pre}outb", 0)
                    p.op("vector", lambda e, a=outb[:], b=qtg[:, ts(oc, TG)]: e.tensor_copy(a, b),
                         reads=[kq], writes=[ko])
                    p.dma("scalar", APx(outd, oc * 128 * S + tg * TG, [[S, 128], [1, TG]]),
                          outb[:], reads=[ko], writes=[("VD", (oc, tg))])
                continue
            for oc in range(NCH):
                p.op("gpsimd", lambda e, a=sqb[:, ts(oc, TG)], b=qtg[:, ts(oc, TG)]: e.tensor_copy(a, b),
                     reads=[(f"{pre}qtg", oc)], writes=[(f"{pre}sqb", oc)])
            ones_sum(lambda c: (f"{pre}sqb", c), sqb, s2r, (f"{pre}s2", 0))
            p.op("vector", lambda e: e.tensor_scalar_mul(s2r, s2r, 1.0 / D),
                 reads=[(f"{pre}s2", 0)], writes=[(f"{pre}s2", 0)])
            for oc in range(NCH):
                kq = (f"{pre}qtg", oc)
                p.op("vector", lambda e, a=qtg[:, ts(oc, TG)]: e.tensor_sub(a, a, s2r),
                     reads=[kq, (f"{pre}s2", 0)], writes=[kq])
                p.op("gpsimd", lambda e, a=sqb[:, ts(oc, TG)], b=qtg[:, ts(oc, TG)]:
                     e.tensor_mul(a, b, b),
                     reads=[kq], writes=[(f"{pre}sqb", oc)])
            ones_sum(lambda c: (f"{pre}sqb", c), sqb, r2r, (f"{pre}r2", 0))
            p.op("vector", lambda e: e.tensor_scalar_add(r2r, r2r, EPS_SUM),
                 reads=[(f"{pre}r2", 0)], writes=[(f"{pre}r2", 0)])
            p.op("vector", lambda e: e.reciprocal(r2r, r2r),
                 reads=[(f"{pre}r2", 0)], writes=[(f"{pre}r2", 0)])
            for oc in range(NCH):
                kq = (f"{pre}qtg", oc)
                p.op("vector", lambda e, a=qtg[:, ts(oc, TG)], b=sqb[:, ts(oc, TG)]:
                     e.tensor_mul(a, b, r2r),
                     reads=[(f"{pre}sqb", oc), (f"{pre}r2", 0)], writes=[kq])
            if part == 0:
                for oc in range(NCH):
                    p.op("gpsimd", lambda e, a=sqb[:, ts(oc, TG)], b=qtg[:, ts(oc, TG)]:
                         e.tensor_mul(a, b, b),
                         reads=[(f"{pre}qtg", oc)], writes=[(f"{pre}sqb", oc)])
                ones_sum(lambda c: (f"{pre}sqb", c), sqb, rlr, (f"{pre}rl", 0))
                p.op("scalar", lambda e: e.activation(rlr, rlr, AF.Ln),
                     reads=[(f"{pre}rl", 0)], writes=[(f"{pre}rl", 0)])
                p.op("scalar", lambda e: e.activation(rlr, rlr, AF.Exp, scale=0.5),
                     reads=[(f"{pre}rl", 0)], writes=[(f"{pre}rl", 0)])
                p.op("vector", lambda e: e.tensor_scalar_max(rlr, rlr, EPS_NORM),
                     reads=[(f"{pre}rl", 0)], writes=[(f"{pre}rl", 0)])
                p.op("vector", lambda e: e.reciprocal(rlr, rlr),
                     reads=[(f"{pre}rl", 0)], writes=[(f"{pre}rl", 0)])
                for oc in range(NCH):
                    ko = (f"{pre}outb", 0)
                    p.op("vector", lambda e, a=outb[:], b=qtg[:, ts(oc, TG)]: e.tensor_mul(a, b, rlr),
                         reads=[(f"{pre}qtg", oc), (f"{pre}rl", 0)], writes=[ko])
                    p.dma("scalar", APx(outd, oc * 128 * S + tg * TG, [[S, 128], [1, TG]]),
                          outb[:], reads=[ko], writes=[("Q2D", (oc, tg))])
            else:
                for oc in range(NCH):
                    ko = (f"{pre}outb", 0)
                    p.op("vector", lambda e, a=outb[:], b=qtg[:, ts(oc, TG)]: e.tensor_copy(a, b),
                         reads=[(f"{pre}qtg", oc)], writes=[ko])
                    p.dma("scalar", APx(outd, oc * 128 * S + tg * TG, [[S, 128], [1, TG]]),
                          outb[:], reads=[ko], writes=[("K2D", (oc, tg))])
                    kq = (f"{pre}qtg", oc)
                    p.op("vector", lambda e, a=qtg[:, ts(oc, TG)]: e.tensor_mul(a, a, a),
                         reads=[kq], writes=[kq])
                    p.op("vector", lambda e, a=nacc[:, 8 + (oc % 2):9 + (oc % 2)],
                         b=qtg[:, ts(oc, TG)]: e.reduce_sum(a, b, axis=AX.X),
                         reads=[kq], writes=[(f"{pre}nacc", ("t", oc % 2))])
                    p.op("vector", lambda e, a=nacc[:, oc:oc + 1],
                         b=nacc[:, 8 + (oc % 2):9 + (oc % 2)]: e.tensor_add(a, a, b),
                         reads=[(f"{pre}nacc", ("t", oc % 2)), (f"{pre}nacc", "*")],
                         writes=[(f"{pre}nacc", "*")])
    p.op("scalar", lambda e: e.activation(nacc[:, 0:8], nacc[:, 0:8], AF.Ln),
         reads=[(f"{pre}nacc", "*")], writes=[(f"{pre}nacc", "*")])
    p.op("scalar", lambda e: e.activation(nacc[:, 0:8], nacc[:, 0:8], AF.Exp, scale=0.5),
         reads=[(f"{pre}nacc", "*")], writes=[(f"{pre}nacc", "*")])
    p.op("vector", lambda e: e.tensor_scalar_max(nacc[:, 0:8], nacc[:, 0:8], EPS_NORM),
         reads=[(f"{pre}nacc", "*")], writes=[(f"{pre}nacc", "*")])
    p.op("vector", lambda e: e.reciprocal(nacc[:, 0:8], nacc[:, 0:8]),
         reads=[(f"{pre}nacc", "*")], writes=[(f"{pre}nacc", "*")])
    for tg in range(NTG):
        for oc in range(NCH):
            ki = ko = (f"{pre}outb", 0)
            p.dma("sync", outb[:], APx(k2_d, oc * 128 * S + tg * TG, [[S, 128], [1, TG]]),
                  reads=[("K2D", (oc, tg))], writes=[ki])
            p.op("vector", lambda e, r=nacc[:, oc:oc + 1]: e.tensor_scalar_mul(outb[:], outb[:], r),
                 reads=[ki, (f"{pre}nacc", "*")], writes=[ko])
            p.dma("scalar", APx(k2f_d, oc * 128 * S + tg * TG, [[S, 128], [1, TG]]),
                  outb[:], reads=[ko], writes=[("K2FD", (oc, tg))])
    p.barrier()
    stA.close()
    if PH == "A":
        return

    stB = ExitStack()
    ktile = stB.enter_context(nc.sbuf_tensor(f"{pre}_ktile", [128, 1024], BF16))
    vtile = stB.enter_context(nc.sbuf_tensor(f"{pre}_vtile", [128, 1024], BF16))
    kT = stB.enter_context(nc.sbuf_tensor(f"{pre}_kT", [128, 1024], BF16))
    vTt = stB.enter_context(nc.sbuf_tensor(f"{pre}_vTt", [128, 1024], BF16))
    kvt = stB.enter_context(nc.sbuf_tensor(f"{pre}_kvt", [128, TG], F32))
    for i in range(16):
        p.op("vector", lambda e, a=kv_sb[:, ts(i, TG)]: e.memset(a, 0.0),
             writes=[(f"{pre}kvsb", i)])
    for tc in range(32):
        kki, kvi = (f"{pre}kti", 0), (f"{pre}vti", 0)
        p.dma("sync", ktile[:].rearrange("p (c f) -> p c f", f=128),
              APx(k2f_d, tc * 128, [[S, 128], [128 * S, 8], [1, 128]]),
              reads=[("K2FD", (c_, tc * 128 // TG)) for c_ in range(NCH)], writes=[kki])
        p.dma("gpsimd", vtile[:].rearrange("p (c f) -> p c f", f=128),
              APx(v_d, tc * 128, [[S, 128], [128 * S, 8], [1, 128]]),
              reads=[("VD", (c_, tc * 128 // TG)) for c_ in range(NCH)], writes=[kvi])
        ctx.batch_transpose(p, [ktile[:, ts(i)] for i in range(8)], [kki],
                            kT[:], (f"{pre}kT", 0), 0)
        ctx.batch_transpose(p, [vtile[:, ts(i)] for i in range(8)], [kvi],
                            vTt[:], (f"{pre}vTt", 0), 1)
        for cc in range(NCH):
            for dg in range(2):
                pb, kpb = ctx.ps(4 + dg)
                p.mm(pb[:], kT[:, ts(cc)], vTt[:, dg * 512: (dg + 1) * 512],
                     start=True, stop=True,
                     reads=[(f"{pre}kT", 0), (f"{pre}vTt", 0)],
                     writes=[kpb])
                i = cc * 2 + dg
                p.op("vector", lambda e, a=kv_sb[:, ts(i, TG)], b=pb:
                     e.scalar_tensor_tensor(a, b[:], 1.0, a, op0=ALU.mult, op1=ALU.add),
                     reads=[kpb, (f"{pre}kvsb", i)], writes=[(f"{pre}kvsb", i)])
    for i in range(16):
        p.op("vector", lambda e, a=kv_b[:, ts(i, TG)], b=kv_sb[:, ts(i, TG)]: e.tensor_copy(a, b),
             reads=[(f"{pre}kvsb", i)], writes=[(f"{pre}kvb2", i)])
    p.barrier()
    stB.close()
    if PH == "B":
        return

    stC = ExitStack()
    wlnb = stC.enter_context(nc.sbuf_tensor(f"{pre}_wlnb", [128, NCH * 1024], BF16))
    q2tg = stC.enter_context(nc.sbuf_tensor(f"{pre}_q2tg", [128, NCH * TG], BF16))
    vtg = stC.enter_context(nc.sbuf_tensor(f"{pre}_vtg", [128, NCH * TG], BF16))
    vtb = stC.enter_context(nc.sbuf_tensor(f"{pre}_vtb", [128, NCH * TG], BF16))
    yt = stC.enter_context(nc.sbuf_tensor(f"{pre}_yt", [128, TG], F32))
    for kc in range(NCH):
        p.dma("sync", wlnb[:, ts(kc, 1024)],
              APx(w["wln"], kc * 128 * 1024, [[1024, 128], [1, 1024]]),
              writes=[(f"{pre}wlnb", kc)])
    kwl_all = [(f"{pre}wlnb", kc) for kc in range(NCH)]
    blnTt, kbln = blnT
    for tg in range(NTG):
        kq2, kvt_ = (f"{pre}q2tg", 0), (f"{pre}vtg", 0)
        p.dma("sync", q2tg[:].rearrange("p (c t) -> p c t", t=TG),
              APx(q2_d, tg * TG, [[S, 128], [128 * S, NCH], [1, TG]]),
              reads=[("Q2D", (c_, tg)) for c_ in range(NCH)], writes=[kq2])
        p.dma("gpsimd", vtg[:].rearrange("p (c t) -> p c t", t=TG),
              APx(v_d, tg * TG, [[S, 128], [128 * S, NCH], [1, TG]]),
              reads=[("VD", (c_, tg)) for c_ in range(NCH)], writes=[kvt_])
        for oc in range(NCH):
            pb, kpb = ctx.ps(2 + oc % 2)
            p.mm_acc(pb[:],
                     [(kv_b[:, cc * 1024 + oc * 128: cc * 1024 + (oc + 1) * 128],
                       q2tg[:, ts(cc, TG)]) for cc in range(NCH)],
                     reads=[(f"{pre}kvb2", i_) for i_ in range(16)] + [kq2],
                     writes=[kpb])
            kvtb = (f"{pre}vtb", oc)
            p.op("vector", lambda e, a=vtb[:, ts(oc, TG)], b=pb, v_=vtg[:, ts(oc, TG)]:
                 e.scalar_tensor_tensor(a, b[:], 1.0 / 64.0, v_, op0=ALU.mult, op1=ALU.add),
                 reads=[kpb, kvt_], writes=[kvtb])
        kvtb_all = [(f"{pre}vtb", o_) for o_ in range(NCH)]
        if PH == "C2":
            continue
        for oc in range(NCH):
            pb, kpb = ctx.ps(4 + oc % 2)
            p.mm_acc(pb[:],
                     [(wlnb[:, kc * 1024 + oc * 128: kc * 1024 + (oc + 1) * 128],
                       vtb[:, ts(kc, TG)]) for kc in range(NCH)],
                     reads=kwl_all + kvtb_all, writes=[kpb])
            kyt = (f"{pre}yt", 0)
            p.op("vector", lambda e, b=pb, bb=blnTt[:, oc:oc + 1]:
                 e.tensor_scalar_add(yt[:], b[:], bb),
                 reads=[kpb, kbln], writes=[kyt])
            if PH == "C4":
                p.dma("sync", xout_ap[ts(oc), tg * TG:(tg + 1) * TG], yt[:],
                      reads=[kyt], writes=[("XOUT", (oc, tg * TG))])
            else:
                emit_residual_store(ctx, pre + "P", 0, yt[:], kyt, xin_ap, xout_ap,
                                    oc, tg * TG, TG, (epx, epy), xin_key=xin_key,
                                    xq="scalar")
    stC.close()



def l0a_arrs(inp):
    import math
    s, e = 4096, 1024
    pos = np.arange(s, dtype=np.float32)[:, None]
    div = np.exp(np.arange(0, e, 2, dtype=np.float32) * np.float32(-math.log(10000.0) / e))
    pe = np.zeros((s, e), np.float32)
    pe[:, 0::2] = np.sin(pos * div, dtype=np.float32)
    pe[:, 1::2] = np.cos(pos * div, dtype=np.float32)
    peT = np.ascontiguousarray(pe.reshape(s, 8, 128).transpose(1, 2, 0))
    return dict(ln_gamma=inp["L0_attn_a_ln_gamma"], ln_beta=inp["L0_attn_a_ln_beta"],
                wq=inp["L0_attn_a_Wq"], bq=inp["L0_attn_a_bq"],
                wk=inp["L0_attn_a_Wk"], bk=inp["L0_attn_a_bk"],
                wv=inp["L0_attn_a_Wv"], bv=inp["L0_attn_a_bv"],
                wp=inp["L0_attn_a_Wp"], ub=inp["L0_attn_a_u_bias"],
                vb=inp["L0_attn_a_v_bias"], wo=inp["L0_attn_a_Wo"],
                bo=inp["L0_attn_a_bo"], peT=peT)


def emit_attn_a0(ctx, xin_ap, xout_ap, w, stack, pre, xin_key="XIN"):
    nc, p = ctx.nc, ctx.p
    SC = 1.0 / 32.0

    gT = load_vecT(ctx, f"{pre}gT", w["ln_gamma"], 0, 1024, stack)
    bT = load_vecT(ctx, f"{pre}bT", w["ln_beta"], 0, 1024, stack)
    boT = load_vecT(ctx, f"{pre}boT", w["bo"], 0, 1024, stack)
    bqT = load_vecT(ctx, f"{pre}bqT", w["bq"], 0, 128, stack)
    bkT = load_vecT(ctx, f"{pre}bkT", w["bk"], 0, 128, stack)
    bvT = load_vecT(ctx, f"{pre}bvT", w["bv"], 0, 128, stack)
    ubTs = [load_vecT(ctx, f"{pre}ubT{h}", w["ub"], h * 128, 128, stack) for h in range(8)]
    vbTs = [load_vecT(ctx, f"{pre}vbT{h}", w["vb"], h * 128, 128, stack) for h in range(8)]
    bcomb = stack.enter_context(nc.sbuf_tensor(f"{pre}_bcomb", [128, 16], F32))
    for h in range(8):
        uT, ku = ubTs[h]
        vT_, kv = vbTs[h]
        bqTt, kbq = bqT
        p.op("vector", lambda e, a=bcomb[:, h:h + 1], u=uT[:]: e.tensor_add(a, bqTt[:], u),
             reads=[kbq, ku], writes=[(f"{pre}bc", h)])
        p.op("vector", lambda e, a=bcomb[:, 8 + h:9 + h], v_=vT_[:]: e.tensor_add(a, bqTt[:], v_),
             reads=[kbq, kv], writes=[(f"{pre}bc", 8 + h)])

    xnd = nc.dram_tensor(f"{pre}_xnd", [D * S], BF16, kind="Internal")
    stP = ExitStack()
    emit_ln_prologue(ctx, xin_ap, stP, pre + "L", gamma=gT, beta=bT,
                     fold_double=True, xin_key=xin_key, spill=xnd)
    p.barrier()
    stP.close()
    pad_d = nc.dram_tensor(f"{pre}_padd", [S * (S + 1)], BF16, kind="Internal")
    ctx_d = nc.dram_tensor(f"{pre}_ctxd", [D * S], BF16, kind="Internal")

    wsm_f = stack.enter_context(nc.sbuf_tensor(f"{pre}_wsmf", [128, 128], F32))
    wsm = stack.enter_context(nc.sbuf_tensor(f"{pre}_wsm", [128, 4 * 128], BF16))
    for wi, wn in enumerate(("wq", "wk", "wv", "wp")):
        p.dma("sync", wsm_f[:], APx(w[wn], 0, [[128, 128], [1, 128]]),
              writes=[(f"{pre}wsmf", 0)])
        p.op("vector", lambda e, a=wsm[:, ts(wi)]: e.tensor_copy(a, wsm_f[:]),
             reads=[(f"{pre}wsmf", 0)], writes=[(f"{pre}wsm", wi)])

    peb = stack.enter_context(nc.sbuf_tensor(f"{pre}_peb", [128, S], BF16))
    pb_ = stack.enter_context(nc.sbuf_tensor(f"{pre}_pb", [128, S], BF16))
    xnh = stack.enter_context(nc.sbuf_tensor(f"{pre}_xnh", [128, S], BF16))
    qu = stack.enter_context(nc.sbuf_tensor(f"{pre}_qu", [128, S], BF16))
    qv = stack.enter_context(nc.sbuf_tensor(f"{pre}_qv", [128, S], BF16))
    kb = stack.enter_context(nc.sbuf_tensor(f"{pre}_kb", [128, S], BF16))
    vb_ = stack.enter_context(nc.sbuf_tensor(f"{pre}_vb", [128, S], BF16))
    vT = stack.enter_context(nc.sbuf_tensor(f"{pre}_vT", [128, S], BF16))
    ab = stack.enter_context(nc.sbuf_tensor(f"{pre}_ab", [128, S + 8], BF16))
    shb = stack.enter_context(nc.sbuf_tensor(f"{pre}_shb", [128, 2 * S], BF16))
    prb = stack.enter_context(nc.sbuf_tensor(f"{pre}_prb", [128, 4 * S], BF16))
    ls0 = stack.enter_context(nc.sbuf_tensor(f"{pre}_ls0", [128, 16], F32))
    cm8 = stack.enter_context(nc.sbuf_tensor(f"{pre}_cm8", [128, 1], F32))
    p.op("vector", lambda e: e.memset(cm8[:], -8.0), writes=[(f"{pre}cm8", "*")])
    pT = stack.enter_context(nc.sbuf_tensor(f"{pre}_pT", [128, S], BF16))
    mm_ = stack.enter_context(nc.sbuf_tensor(f"{pre}_mm", [128, 8], F32))
    cxe = stack.enter_context(nc.sbuf_tensor(f"{pre}_cxe", [128, 256], BF16))
    p.op("vector", lambda e: e.memset(ab[:, 0:1], 0.0), writes=[(f"{pre}abz", "*")])

    for h in range(8):
        kxh = (f"{pre}xnh", 0)
        p.dma("sync", peb[:], APx(w["peT"], h * 128 * S, [[S, 128], [1, S]]),
              writes=[(f"{pre}peb", 0)])
        p.dma("sync", xnh[:], APx(xnd, h * 128 * S, [[S, 128], [1, S]]),
              reads=[("XND", (tg_, h)) for tg_ in range(NTG)], writes=[kxh])
        for tg in range(NTG):
            pbk, kpb = ctx.ps(2 + tg % 2)
            p.mm(pbk[:], wsm[:, ts(3)], peb[:, ts(tg, TG)], start=True, stop=True,
                 reads=[(f"{pre}wsm", 3), (f"{pre}peb", 0)], writes=[kpb])
            p.op("vector", lambda e, a=pb_[:, ts(tg, TG)], b=pbk: e.tensor_copy(a, b[:]),
                 reads=[kpb], writes=[(f"{pre}pb", tg)])
            pbq, kpq = ctx.ps(4)
            p.mm(pbq[:], wsm[:, ts(0)], xnh[:, ts(tg, TG)], start=True, stop=True,
                 reads=[(f"{pre}wsm", 0), kxh], writes=[kpq])
            p.op("vector", lambda e, a=qu[:, ts(tg, TG)], b=pbq, bb=bcomb[:, h:h + 1]:
                 e.tensor_scalar_add(a, b[:], bb),
                 reads=[kpq, (f"{pre}bc", h)], writes=[(f"{pre}qu", tg)])
            p.op("vector", lambda e, a=qv[:, ts(tg, TG)], b=pbq, bb=bcomb[:, 8 + h:9 + h]:
                 e.tensor_scalar_add(a, b[:], bb),
                 reads=[kpq, (f"{pre}bc", 8 + h)], writes=[(f"{pre}qv", tg)])
            pbk2, kpk = ctx.ps(5)
            p.mm(pbk2[:], wsm[:, ts(1)], xnh[:, ts(tg, TG)], start=True, stop=True,
                 reads=[(f"{pre}wsm", 1), kxh], writes=[kpk])
            bkTt, kbk = bkT
            p.op("vector", lambda e, a=kb[:, ts(tg, TG)], b=pbk2, bb=bkTt[:]:
                 e.tensor_scalar_add(a, b[:], bb),
                 reads=[kpk, kbk], writes=[(f"{pre}kb", tg)])
            pbv2, kpv = ctx.ps(2 + tg % 2)
            p.mm(pbv2[:], wsm[:, ts(2)], xnh[:, ts(tg, TG)], start=True, stop=True,
                 reads=[(f"{pre}wsm", 2), kxh], writes=[kpv])
            bvTt, kbv = bvT
            p.op("vector", lambda e, a=vb_[:, ts(tg, TG)], b=pbv2, bb=bvTt[:]:
                 e.tensor_scalar_add(a, b[:], bb),
                 reads=[kpv, kbv], writes=[(f"{pre}vbv", tg)])
        for g in range(4):
            ctx.batch_transpose(p, [vb_[:, ts(8 * g + i)] for i in range(8)],
                                [(f"{pre}vbv", t_) for t_ in range(NTG)],
                                vT[:, g * 1024:(g + 1) * 1024], (f"{pre}vT", g), g)
        kvT_all = [(f"{pre}vT", g) for g in range(4)]
        kqu_all = [(f"{pre}qu", t) for t in range(NTG)]
        kqv_all = [(f"{pre}qv", t) for t in range(NTG)]
        kkb_all = [(f"{pre}kb", t) for t in range(NTG)]
        for qt in range(32):
            for jg in range(NTG):
                pbs, kpbs = ctx.ps(2 + jg % 2)
                p.mm(pbs[:], qv[:, ts(qt)], pb_[:, ts(jg, TG)], start=True, stop=True,
                     reads=kqv_all + [(f"{pre}pb", jg)], writes=[kpbs])
                p.op("vector", lambda e, a=ab[:, 1 + jg * TG: 1 + (jg + 1) * TG], b=pbs:
                     e.tensor_copy(a, b[:]),
                     reads=[kpbs], writes=[(f"{pre}ab", jg)])
            p.dma("scalar", APx(pad_d, qt * 128 * (S + 1), [[S + 1, 128], [1, S + 1]]),
                  ab[:, 0:S + 1],
                  reads=[(f"{pre}ab", j) for j in range(NTG)] + [(f"{pre}abz", "*")],
                  writes=[("PADD", qt)])
        def stageA0(qt):
            par = qt % 2
            par4 = qt % 4
            shv = shb[:, par * S:(par + 1) * S]
            pv = prb[:, par4 * S:(par4 + 1) * S]
            lsv = ls0[:, par * 8:(par + 1) * 8]
            l_, rl = mm_[:, par * 4 + 2: par * 4 + 3], mm_[:, par * 4 + 3: par * 4 + 4]
            ksh = (f"{pre}shb", par)
            p.dma("sync", shv, APx(pad_d, (qt * 128 + 1) * S, [[S, 128], [1, S]]),
                  reads=[("PADD", q_) for q_ in range(qt, min(qt + 2, 32))], writes=[ksh])
            for jg in range(NTG):
                pbs, kpbs = ctx.ps(2 + jg % 3)
                p.mm(pbs[:], qu[:, ts(qt)], kb[:, ts(jg, TG)], start=True, stop=False,
                     reads=kqu_all + kkb_all, writes=[kpbs])
                p.mm(pbs[:], ctx.id_bf[:], shv[:, ts(jg, TG)], start=False, stop=True,
                     reads=[ksh, ctx.KIDB], writes=[kpbs])
                p.op("scalar", lambda e, a=pv[:, ts(jg, TG)], b=pbs,
                     c=lsv[:, jg:jg + 1]: e.activation(a, b[:], AF.Exp, bias=cm8[:],
                                                       scale=SC, accum_out=c),
                     reads=[kpbs, (f"{pre}cm8", "*")],
                     writes=[(f"{pre}prb", par4), (f"{pre}ls0", (par, jg))])
            p.op("vector", lambda e, a=l_, s=lsv: e.reduce_sum(a, s, axis=AX.X),
                 reads=[(f"{pre}ls0", (par, j)) for j in range(NTG)],
                 writes=[(f"{pre}m", (par, 2))])
            p.op("vector", lambda e, a=rl, b=l_: e.reciprocal(a, b),
                 reads=[(f"{pre}m", (par, 2))], writes=[(f"{pre}m", (par, 3))])
            p.op("vector", lambda e, a=pv, b=rl: e.tensor_scalar_mul(a, a, b),
                 reads=[(f"{pre}prb", par4), (f"{pre}m", (par, 3))], writes=[(f"{pre}prb", par4)])

        def stageB0(qt):
            par = qt % 2
            par4 = qt % 4
            pv = prb[:, par4 * S:(par4 + 1) * S]
            cx = cxe[:, par * 128:(par + 1) * 128]
            for g in range(4):
                ctx.batch_transpose(p, [pv[:, ts(8 * g + i)] for i in range(8)],
                                    [(f"{pre}prb", par4)],
                                    pT[:, g * 1024:(g + 1) * 1024], (f"{pre}pT", g), g)
            pbc, kpbc = ctx.ps(5)
            p.mm_acc(pbc[0:128, 0:128],
                     [(vT[:, ts(j)], pT[:, ts(j)]) for j in range(32)],
                     reads=kvT_all + [(f"{pre}pT", g) for g in range(4)],
                     writes=[kpbc])
            p.op("vector", lambda e, a=cx, b=pbc: e.tensor_copy(a, b[0:128, 0:128]),
                 reads=[kpbc], writes=[(f"{pre}cxe", par)])
            p.dma("sync", APx(ctx_d, h * 128 * S + qt * 128, [[S, 128], [1, 128]]),
                  cx, reads=[(f"{pre}cxe", par)], writes=[("CTXD", (h, qt))])

        for it in range(34):
            if it >= 2:
                stageB0(it - 2)
            if it < 32:
                stageA0(it)

    emit_proj_residual(ctx, pre, w["wo"], 0, boT, ctx_d,
                       [("CTXD", (h_, qt_)) for h_ in range(8) for qt_ in range(32)],
                       xin_ap, xout_ap, stack, xin_key=xin_key)



def emit_weight_prep(ctx, items, stack):
    nc, p = ctx.nc, ctx.p
    CH = 128 * 8192
    stg = [stack.enter_context(nc.sbuf_tensor(f"wp_stg{i}", [128, 8192], F32))
           for i in range(2)]
    stb = [stack.enter_context(nc.sbuf_tensor(f"wp_stb{i}", [128, 8192], BF16))
           for i in range(2)]
    out = {}
    ci = 0
    for h, numel in items:
        assert numel % CH == 0, (h.name, numel)
        bh = nc.dram_tensor(f"bw_{h.name}", [numel], BF16, kind="Internal")
        out[h.name] = bh
        for c in range(numel // CH):
            b = ci % 2
            kin, kb = ("wpstg", b), ("wpstb", b)
            p.dma("sync" if ci % 2 == 0 else "gpsimd",
                  stg[b][:], APx(h, c * CH, [[8192, 128], [1, 8192]]),
                  writes=[kin])
            eng = "vector" if ci % 2 == 0 else "gpsimd"
            p.op(eng, lambda e, a=stb[b][:], s=stg[b][:]: e.tensor_copy(a, s),
                 reads=[kin], writes=[kb])
            p.dma("scalar", APx(bh, c * CH, [[8192, 128], [1, 8192]]), stb[b][:],
                  reads=[kb], writes=[("PREP", h.name)])
            ci += 1
    return out




def _weight_arrays(inputs):
    inp = inputs
    specs = {}

    def add(prefix, d):
        for k, v in d.items():
            specs[f"{prefix}_{k}"] = np.ascontiguousarray(np.asarray(v, dtype=np.float32))

    def sw_up(w):
        w = np.asarray(w, dtype=np.float32)
        shp = w.shape
        w = w.reshape(-1, 8, 128, 32, 128).transpose(0, 3, 2, 1, 4)
        return np.ascontiguousarray(w.reshape(-1))

    def sw_dn(w):
        w = np.asarray(w, dtype=np.float32)
        w = w.reshape(-1, 32, 128, 8, 128).transpose(0, 3, 2, 1, 4)
        return np.ascontiguousarray(w.reshape(-1))

    add("a0", l0a_arrs(inp))
    add("f0", dict(gate_w=inp["L0_ffn_gate_w"], gate_b=inp["L0_ffn_bias"],
                   up_w=sw_up(inp["L0_ffn_up_w"]), up_b=inp["L0_ffn_up_b"],
                   down_w=sw_dn(inp["L0_ffn_down_w"]), down_b=inp["L0_ffn_down_b"],
                   sw_up=sw_up(inp["L0_ffn_sw_up"]), sb_up=inp["L0_ffn_sb_up"],
                   sw_down=sw_dn(inp["L0_ffn_sw_down"]), sb_down=inp["L0_ffn_sb_down"]))
    add("b0", l0b_arrs(inp))
    add("a1", l1a_arrs(inp))
    add("f1", dict(gate_w=np.ascontiguousarray(inp["L1_ffn_gate_w"].T),
                   gate_b=inp["L1_ffn_gate_b"],
                   up_w=sw_up(inp["L1_ffn_w1"]), up_b=inp["L1_ffn_b1"],
                   down_w=sw_dn(inp["L1_ffn_w2"]), down_b=inp["L1_ffn_b2"],
                   sw_up=sw_up(inp["L1_ffn_sw1"]), sb_up=inp["L1_ffn_sb1"],
                   sw_down=sw_dn(inp["L1_ffn_sw2"]), sb_down=inp["L1_ffn_sb2"]))
    add("b1", l1b_arrs(inp))
    return specs


SUB_FAMILY = ["a0", "f0", "b0", "f0", "a1", "f1", "b1", "f1"]
GROUPS = [[0, 1, 2, 3, 4, 5, 6, 7]]


def _family_names(specs, fam):
    pre = fam + "_"
    return {k[len(pre):]: k for k in specs if k.startswith(pre)}


def _build_program(group, specs):
    nc = bass.Bass("TRN2", target_bir_lowering=False)
    xin_d = nc.dram_tensor("xin", [1024, 4096], F32, kind="ExternalInput")
    y_d = nc.dram_tensor("y", [1024, 4096], F32, kind="ExternalOutput")
    ping = nc.dram_tensor("x_ping", [1024, 4096], F32, kind="Internal")
    pong = nc.dram_tensor("x_pong", [1024, 4096], F32, kind="Internal")

    fams = sorted(set(SUB_FAMILY[i] for i in group))
    wd_by_fam = {}
    used_names = []
    for fam in fams:
        names = _family_names(specs, fam)
        wd_by_fam[fam] = {short: nc.dram_tensor(full, list(specs[full].shape), F32,
                                                kind="ExternalInput")
                          for short, full in names.items()}
        used_names.extend(names.values())

    PREP_KEYS = {"f0": ["up_w", "down_w", "sw_up", "sw_down"],
                 "f1": ["up_w", "down_w", "sw_up", "sw_down"],
                 "b0": ["wq", "wm", "w1", "w2"],
                 "a1": ["wqkv", "wln"],
                 "b1": ["wq", "wk", "wv", "wo"],
                 "a0": ["wo", "peT"]}
    p = Prog(nc)
    from contextlib import ExitStack
    with ExitStack() as stack:
        ctx = Ctx(nc, p, stack)
        with ExitStack() as wps:
            items, slots = [], []
            for fam in fams:
                for kk in PREP_KEYS.get(fam, []):
                    h = wd_by_fam[fam][kk]
                    items.append((h, int(np.prod(specs[h.name].shape))))
                    slots.append((fam, kk))
            bmap = emit_weight_prep(ctx, items, wps)
            for (fam, kk), (h, _) in zip(slots, items):
                wd_by_fam[fam][kk] = bmap[h.name]
            p.barrier()
        cur = xin_d
        bufs = [ping, pong]
        for gi, i in enumerate(group):
            nxt = y_d if gi == len(group) - 1 else bufs[gi % 2]
            fam = SUB_FAMILY[i]
            wd = wd_by_fam[fam]
            pre = f"s{i}"
            with ExitStack() as sls:
                if fam == "a0":
                    emit_attn_a0(ctx, cur.ap(), nxt.ap(), wd, sls, pre)
                elif fam == "f0":
                    emit_ffn(ctx, cur.ap(), nxt.ap(), wd, sls, pre, "L0")
                elif fam == "b0":
                    emit_attn_b0(ctx, cur.ap(), nxt.ap(), wd, sls, pre)
                elif fam == "a1":
                    emit_attn_a1(ctx, cur.ap(), nxt.ap(), wd, sls, pre)
                elif fam == "f1":
                    emit_ffn(ctx, cur.ap(), nxt.ap(), wd, sls, pre, "L1")
                elif fam == "b1":
                    emit_attn_b1(ctx, cur.ap(), nxt.ap(), wd, sls, pre)
                p.barrier()
            cur = nxt
        block = stack.enter_context(nc.Block())
        p.finalize(block)
    return nc, used_names


def kernel(**inputs):
    specs = _weight_arrays(inputs)
    x = np.asarray(inputs["L0_attn_a_inputs"], dtype=np.float32)
    cur = [np.ascontiguousarray(x[i].T) for i in range(8)]
    for group in GROUPS:
        nc, used = _build_program(group, specs)
        in_maps = []
        for core in range(8):
            m = {n: specs[n] for n in used}
            m["xin"] = cur[core]
            in_maps.append(m)
        res = run_bass_kernel_spmd(nc, in_maps, core_ids=list(range(8)))
        cur = [np.asarray(res.results[c]["y"]) for c in range(8)]
    return np.stack([c.T for c in cur]).astype(np.float32)
